# version 09
# baseline (speedup 1.0000x reference)
import math
import numpy as np
import concourse.bass as bass
import concourse.mybir as mybir
from concourse.bass import IndirectOffsetOnAxis
from concourse.bass_utils import run_bass_kernel_spmd

F32 = mybir.dt.float32
BF16 = mybir.dt.bfloat16
I32 = mybir.dt.int32
AF = mybir.ActivationFunctionType
ALU = mybir.AluOpType
AX = mybir.AxisListType

T = 2048
C = 1024
QL = 96
KVL = 32
DH = 4096
SCALE = 1.0 / math.sqrt(96.0)
MAGIC = 1.5 * (2.0 ** 23)
TWO_PI = 2.0 * math.pi
CW1 = float(np.float32(6.28125))
CW2 = float(np.float32(TWO_PI - 6.28125))
CW3 = float(np.float64(TWO_PI) - np.float64(CW1) - np.float64(CW2))
EPS_RMS = 1e-6
EPS_LN = 1e-5

ENGINES = ("sync", "tensor", "vector", "scalar", "gpsimd")
_CACHE = {}


class _Done(Exception):
    def __init__(self, nc):
        self.nc = nc


def build_program():
    try:
        return _build_impl()
    except _Done as d:
        return d.nc


def _build_impl():
    import os
    _stop = os.environ.get("KSTOP", "")
    nc = bass.Bass("TRN2", target_bir_lowering=False)

    def din(name, shape, dt=F32):
        return nc.dram_tensor(name, list(shape), dt, kind="ExternalInput").ap()

    x_tok = din("x_tok", [T, C])
    xT_d = din("xT", [C, T])
    w_sa = din("w_sa", [C, 128])
    w_sb = din("w_sb", [C, 68])
    wdq_d = din("wdq", [QL, 1536])
    wdk_d = din("wdk", [KVL, 512])
    wdv_d = din("wdv", [KVL, 512])
    wwink_d = din("wwink", [C, 1536])
    wwinv_d = din("wwinv", [C, 512])
    wselk_d = din("wselk", [C, 1536])
    wselv_d = din("wselv", [C, 512])
    wproj_d = din("wproj", [512, C])
    w1T_d = din("w1T", [C, DH])
    w2T_d = din("w2T", [DH, C])
    wimp_d = din("wimp", [C, 1])
    gq_d = din("gq", [QL, 1])
    gkv_d = din("gkv", [KVL, 1])
    cw_d = din("cwcol", [128, 96])
    cb_d = din("cbcol", [128, 32])
    c_trow = din("c_trow", [1, T])
    c_drow = din("c_drow", [1, 128])
    c_ident = din("c_ident", [128, 128])
    c_iota256 = din("c_iota256", [128, 256])
    c_tcol = din("c_tcol", [128, 16])
    c_masks = din("c_masks", [128, 2048], BF16)
    c_ones = din("c_ones", [1, 128])

    out_dram = nc.dram_tensor("outT", [C, T], F32, kind="ExternalOutput").ap()
    scr_x1 = nc.dram_tensor("scr_x1", [C, T], F32, kind="Internal").ap()
    scr_z2 = nc.dram_tensor("scr_z2", [C, T], F32, kind="Internal").ap()
    scr_kw = nc.dram_tensor("scr_kw", [12, 128, T], BF16, kind="Internal").ap()
    scr_w2 = nc.dram_tensor("scr_w2", [DH, C], BF16, kind="Internal").ap()

    cms = []

    def salloc(name, shape, dt):
        cm = nc.sbuf_tensor(name, list(shape), dt)
        t = cm.__enter__()
        cms.append(cm)
        return t

    BQ = salloc("BQ", [128, 24576], BF16)
    BK = salloc("BK", [128, 12288], BF16)
    BV = salloc("BV", [128, 16384], BF16)
    BS = salloc("BS", [128, 8192], BF16)
    BA = salloc("BA", [128, 8192], BF16)
    FW = salloc("FW", [128, 6144], F32)
    FR = salloc("FR", [128, 4608], F32)
    FK = salloc("FK", [128, 1024], F32)
    FC = salloc("FC", [128, 64], F32)
    IIX = salloc("IIX", [128, 260], I32)

    xTb = [BV[:, 2048 * k:2048 * (k + 1)] for k in range(8)]
    CA = BS[:, 0:2048]
    SA = BS[:, 2048:4096]
    masks = BS[:, 0:2048]
    nqT = BS[:, 4096:6144]
    krT = BS[:, 6144:8192]
    ckvT = BA[:, 0:2048]
    vslots = [BV[:, 528 * j:528 * (j + 1)] for j in range(16)]
    ksall = [BV[:, 8448 + 256 * j:8448 + 256 * (j + 1)] for j in range(12)]
    vsl = [BV[:, 11520 + 528 * j:11520 + 528 * (j + 1)] for j in range(2)]
    PT = [BV[:, 12576 + 512 * j:12576 + 512 * (j + 1)] for j in range(2)]
    selT = [BV[:, 13600 + 256 * k:13600 + 256 * (k + 1)] for k in range(8)]
    Qg = [[BQ[:, 2048 * (3 * g + p):2048 * (3 * g + p + 1)] for p in range(3)]
          for g in range(4)]
    wsA_b = [BK[:, 128 * k:128 * (k + 1)] for k in range(8)]
    wsB_b = [BK[:, 1024 + 128 * k:1024 + 128 * (k + 1)] for k in range(8)]
    wdq_b = BK[:, 2048:3584]
    wink_b = [BK[:, 1536 * k:1536 * (k + 1)] for k in range(8)]
    winv_b = [BK[:, 512 * k:512 * (k + 1)] for k in range(8)]
    wselk_b = [BK[:, 384 * k:384 * (k + 1)] for k in range(8)]
    wselv_b = [BK[:, 3072 + 512 * k:3072 + 512 * (k + 1)] for k in range(8)]
    kn_g = BK[:, 0:2048]
    kr2tmp = BA[:, 4624:6672]
    krx1 = BK[:, 2048:4096]
    krx2 = BK[:, 4096:6144]
    wdk_b = BK[:, 8192:8704]
    wdv_b = BK[:, 8704:9216]
    kwbuf = [[BK[:, 6144 * h + 2048 * p:6144 * h + 2048 * (p + 1)] for p in range(3)]
             for h in range(2)]
    wproj_b = [BK[:, 8192 + 1024 * g:8192 + 1024 * (g + 1)] for g in range(4)]
    agg = [BA[:, 2048 * g:2048 * (g + 1)] for g in range(4)]
    w1q = [BQ[:, 4096 * k:4096 * (k + 1)] for k in range(6)] + \
          [BK[:, 4096 * k:4096 * (k + 1)] for k in range(2)]
    hsl = [BV[:, 512 * m:512 * (m + 1)] for m in range(32)]
    xq = [BA[:, 2048 * k:2048 * (k + 1)] for k in range(4)] + \
         [BS[:, 2048 * k:2048 * (k + 1)] for k in range(4)]
    w2bf = [BK[:, 8192 + 1024 * j:8192 + 1024 * (j + 1)] for j in range(2)]

    fwv = [FW[:, 512 * i:512 * (i + 1)] for i in range(12)]
    fw2k = [FW[:, 2048 * i:2048 * (i + 1)] for i in range(3)]
    h1f = FW[:, 5120 + 0:5120 + 514]
    R = FR[:, 0:2048]
    F0 = FR[0:1, :]
    F32r = FR[32:33, :]
    ROW_TROW = F0[:, 0:2048]
    ROW_F = F0[:, 2048:2176]
    ROW_D = F0[:, 2176:2304]
    ROW_A = F0[:, 0:2048]
    ROW_B = F0[:, 2048:4096]
    ROW_MEAN = F0[:, 256:768]
    ROW_INV = F0[:, 768:1280]
    ROW_BB = F0[:, 1280:1792]
    ROW_SC = F0[:, 1792:2304]
    ROW_RX = F0[:, 2304:4352]
    ROW_G = F0[:, 4352:4355]
    ROW_S1 = F0[:, 4356:4360]
    ROW_CM = F0[:, 4360:4368]
    sums_mc = FR[:, 4368:4384]
    sums_ic = FR[:, 4384:4400]
    k_ident = FK[:, 0:128]
    k_iota = FK[:, 128:384]
    k_tcol = FK[:, 384:400]
    k_ones = FK[:, 400:528]
    k_cw = FK[:, 528:624]
    k_cb = FK[:, 624:656]
    k_onescol = FK[:, 656:657]
    k_magic = FK[:, 657:658]
    k_negmagic = FK[:, 658:659]

    sems = {}
    for s in ("dma", "gdma", "pe", "dve", "act", "gp"):
        cm = nc.semaphore("sem_" + s)
        sems[s] = cm.__enter__()
        cms.append(cm)

    def palloc(name, shape):
        cm = nc.psum_tensor(name, list(shape), F32)
        t = cm.__enter__()
        cms.append(cm)
        return t

    pBig = palloc("pBig", [128, 2048])
    pB = [pBig[:, 512 * i:512 * (i + 1)] for i in range(4)]
    pS = [palloc("pS0", [128, 512])[:], palloc("pS1", [128, 512])[:]]
    pA = [palloc("pA0", [128, 512])[:], palloc("pA1", [128, 512])[:]]

    q = {e: [] for e in ENGINES}
    cnt = {k: 0 for k in sems}

    def push(eng, waits, build, inc=None, inck=1):
        waits = tuple((s, v) for (s, v) in waits if v > 0)

        def fn(e, waits=waits, build=build, inc=inc, inck=inck):
            for s, v in waits:
                e.wait_ge(sems[s], v)
            ins = build(e)
            if inc is not None and ins is not None:
                ins.then_inc(sems[inc], inck)
        q[eng].append(fn)
        if inc is not None:
            cnt[inc] += inck
        return cnt[inc] if inc else 0

    def at(*ss):
        return [(s, cnt[s]) for s in ss]

    def dma(dst, src, waits=()):
        return push("sync", waits, lambda e: e.dma_start(out=dst, in_=src), "dma", 16)

    def gdma(dst, src, waits=()):
        return push("gpsimd", waits, lambda e: e.dma_start(out=dst, in_=src), "gdma", 16)

    def mm(out, lhsT, rhs, start, stop, tp=None, waits=()):
        return push("tensor", waits,
                    lambda e: nc.tensor.matmul(out, lhsT, rhs, start=start, stop=stop,
                                               tile_position=tp), "pe", 1)

    def tr(out, in_, waits=()):
        return push("tensor", waits, lambda e: nc.tensor.transpose(out, in_, k_ident),
                    "pe", 1)

    def dve(build, waits=()):
        return push("vector", tuple(waits) + (("dve", cnt["dve"]),), build, "dve", 1)

    def act(build, waits=()):
        return push("scalar", waits, build, "act", 1)

    def gp(build, waits=()):
        return push("gpsimd", waits, build, "gp", 1)

    def assemble():
        with nc.Block() as block:
            @block.sync
            def _(eng):
                for fn in q["sync"]:
                    fn(eng)

            @block.tensor
            def _(eng):
                for fn in q["tensor"]:
                    fn(eng)

            @block.vector
            def _(eng):
                for fn in q["vector"]:
                    fn(eng)

            @block.scalar
            def _(eng):
                for fn in q["scalar"]:
                    fn(eng)

            @block.gpsimd
            def _(eng):
                for fn in q["gpsimd"]:
                    fn(eng)
        for cm in reversed(cms):
            try:
                cm.__exit__(None, None, None)
            except Exception:
                pass

    def ckpt(name, aps):
        if _stop != name:
            return
        for nm, ap in aps:
            t = nc.dram_tensor("dbg_" + nm, list(ap.shape), ap.dtype,
                               kind="ExternalOutput").ap()
            dma(t, ap, waits=at("dve", "act", "pe", "gdma", "gp"))
        push("vector", [("dma", cnt["dma"])],
             lambda e: e.memset(FC[:, 63:64], 0.0), "dve", 1)
        assemble()
        raise _Done(nc)

    def bcast_row(pdst_cols, row_ap, waits=()):
        first = True
        for pdst, w0 in pdst_cols:
            W = pdst.shape[-1]
            mm(pdst, k_ones[0:1, :], row_ap[:, w0:w0 + W], True, True,
               waits=waits if first else ())
            first = False
        return cnt["pe"]

    stage_state = {"last": [0, 0]}

    def quant_round(out_ap, src_ap, scale_psum, t0, t1, waits=()):
        dve(lambda e: e.tensor_tensor(out=t0, in0=src_ap, in1=scale_psum, op=ALU.mult),
            waits=waits)
        act(lambda e: e.activation(out=t1, in_=t0, func=AF.Identity, bias=k_magic,
                                   scale=1.0), waits=at("dve"))
        dve(lambda e: e.tensor_copy(out=t0.bitcast(I32), in_=t1.bitcast(I32)),
            waits=at("act"))
        act(lambda e: e.activation(out=out_ap, in_=t0, func=AF.Identity, bias=k_negmagic,
                                   scale=1.0), waits=at("dve"))
        return cnt["act"]

    def load_bf16(dst_bf, src_ap, cols, si, rows=128):
        dk = dma(fw2k[si][0:rows, 0:cols], src_ap, waits=at("dve"))
        dve(lambda e, si=si, cols=cols, rows=rows, dst_bf=dst_bf: e.tensor_copy(
            out=dst_bf, in_=fw2k[si][0:rows, 0:cols]), waits=[("dma", dk)])
        stage_state["last"][si] = cnt["dve"]

    dma(k_ident, c_ident)
    dma(k_iota, c_iota256)
    dma(k_tcol, c_tcol)
    dve(lambda e: e.memset(k_ones, 1.0))
    dma(k_cw, cw_d)
    dma(k_cb, cb_d)
    dma(ROW_TROW, c_trow)
    dma(ROW_D, c_drow)
    dma(FC[0:96, 0:1], gq_d)
    dma(FC[0:32, 1:2], gkv_d)
    c_dma = cnt["dma"]
    dve(lambda e: e.memset(k_onescol, 1.0))
    dve(lambda e: e.memset(k_magic, MAGIC))
    dve(lambda e: e.memset(k_negmagic, -MAGIC))

    cast_done = [0] * 8
    for k in range(8):
        w = [("dve", cast_done[k - 2])] if k >= 2 else []
        dk = dma(fw2k[k % 2][:, :], xT_d[128 * k:128 * (k + 1), :], waits=w)
        dve(lambda e, k=k: e.tensor_copy(out=xTb[k], in_=fw2k[k % 2]), waits=[("dma", dk)])
        cast_done[k] = cnt["dve"]

    act(lambda e: e.activation(out=ROW_F, in_=ROW_D, func=AF.Exp,
                               scale=-math.log(10000.0) / 32.0), waits=[("dma", c_dma)])
    for n in range(4):
        mm(pB[n], ROW_F, ROW_TROW[:, 512 * n:512 * (n + 1)], True, True,
           waits=at("act") + [("dve", cast_done[7])] if n == 0 else ())
    pe_ang = cnt["pe"]
    dve(lambda e: e.tensor_scalar_mul(out=fw2k[2], in0=pBig[:, :], scalar1=1.0 / TWO_PI),
        waits=[("pe", pe_ang)])
    dve(lambda e: e.tensor_scalar_add(out=fw2k[2], in0=fw2k[2], scalar1=MAGIC))
    dve(lambda e: e.tensor_copy(out=fw2k[0].bitcast(I32), in_=fw2k[2].bitcast(I32)))
    dve(lambda e: e.tensor_scalar_add(out=fw2k[0], in0=fw2k[0], scalar1=-MAGIC))
    dve(lambda e: e.tensor_scalar_mul(out=fw2k[1], in0=fw2k[0], scalar1=-CW1))
    dve(lambda e: e.tensor_add(out=fw2k[1], in0=fw2k[1], in1=pBig[:, :]))
    dve(lambda e: e.tensor_scalar_mul(out=fw2k[2], in0=fw2k[0], scalar1=-CW2))
    dve(lambda e: e.tensor_add(out=fw2k[1], in0=fw2k[1], in1=fw2k[2]))
    dve(lambda e: e.tensor_scalar_mul(out=fw2k[2], in0=fw2k[0], scalar1=-CW3))
    dve(lambda e: e.tensor_add(out=fw2k[1], in0=fw2k[1], in1=fw2k[2]))
    ang = cnt["dve"]
    act(lambda e: e.activation(out=SA, in_=fw2k[1], func=AF.Sin, scale=1.0),
        waits=[("dve", ang)])
    dve(lambda e: e.tensor_scalar_add(out=fw2k[2], in0=fw2k[1], scalar1=math.pi / 2.0),
        waits=at("act"))
    dve(lambda e: e.tensor_single_scalar(out=fw2k[0], in_=fw2k[2], scalar=math.pi,
                                         op=ALU.is_gt))
    dve(lambda e: e.tensor_scalar_mul(out=fw2k[0], in0=fw2k[0], scalar1=-TWO_PI))
    dve(lambda e: e.tensor_add(out=fw2k[2], in0=fw2k[2], in1=fw2k[0]))
    act(lambda e: e.activation(out=CA, in_=fw2k[2], func=AF.Sin, scale=1.0),
        waits=at("dve"))
    tables = cnt["act"]
    ckpt("tables", [("CA", CA), ("SA", SA)])

    def rope_cols(out1, out2, x1, x2, ca, sa, w0, W, waits=()):
        P = x1.shape[0]
        a, b = fwv[8][0:P, 0:W], fwv[9][0:P, 0:W]
        s = slice(w0, w0 + W)
        dve(lambda e: e.tensor_tensor(out=a, in0=x1, in1=ca[0:P, s], op=ALU.mult),
            waits=waits)
        dve(lambda e: e.tensor_tensor(out=b, in0=x2, in1=sa[0:P, s], op=ALU.mult))
        dve(lambda e: e.tensor_tensor(out=out1[:, s] if out1.shape[0] == P
                                      else out1[0:P, s],
                                      in0=a, in1=b, op=ALU.subtract))
        dve(lambda e: e.tensor_tensor(out=a, in0=x1, in1=sa[0:P, s], op=ALU.mult))
        dve(lambda e: e.tensor_tensor(out=b, in0=x2, in1=ca[0:P, s], op=ALU.mult))
        dve(lambda e: e.tensor_tensor(out=out2[:, s] if out2.shape[0] == P
                                      else out2[0:P, s],
                                      in0=a, in1=b, op=ALU.add))
        return cnt["dve"]

    for k in range(8):
        load_bf16(wsA_b[k], w_sa[128 * k:128 * (k + 1), :], 128, k % 2)
    wsa = cnt["dve"]
    for tq in range(4):
        for k in range(8):
            mm(pB[tq], wsA_b[k], xTb[k][:, 512 * tq:512 * (tq + 1)], k == 0, k == 7,
               waits=[("dve", wsa)] if (tq == 0 and k == 0) else ())
    peA = cnt["pe"]
    dve(lambda e: e.tensor_copy(out=fw2k[0][0:96, :], in_=pBig[0:96, :]),
        waits=[("pe", peA)])
    dve(lambda e: e.tensor_copy(out=fw2k[1][0:32, :], in_=pBig[96:128, :]))
    rawAB = cnt["dve"]

    def rms_norm(out_bf, src_f32, P, g_col, waits=()):
        act(lambda e: e.activation(out=fw2k[2][0:P, :], in_=src_f32, func=AF.Square,
                                   scale=1.0), waits=waits)
        sq = cnt["act"]
        for tq in range(4):
            m_ = mm(pA[0][0:1, :], k_onescol[0:P, :], fw2k[2][0:P, 512 * tq:512 * (tq + 1)],
                    True, True, waits=[("act", sq), ("dve", cnt["dve"])])
            dve(lambda e, tq=tq: e.tensor_copy(out=ROW_A[:, 512 * tq:512 * (tq + 1)],
                                               in_=pA[0][0:1, :]), waits=[("pe", m_)])
        dve(lambda e, P=P: e.tensor_scalar_mul(out=ROW_A, in0=ROW_A, scalar1=1.0 / P))
        dve(lambda e: e.tensor_scalar_add(out=ROW_A, in0=ROW_A, scalar1=EPS_RMS))
        act(lambda e: e.activation(out=ROW_B[:, 0:2048], in_=ROW_A, func=AF.Sqrt,
                                   scale=1.0), waits=at("dve"))
        dve(lambda e: e.reciprocal(out=ROW_B[:, 0:2048], in_=ROW_B[:, 0:2048]),
            waits=at("act"))
        b_ = bcast_row([(pB[i], 512 * i) for i in range(4)], ROW_B[:, 0:2048],
                       waits=at("dve"))
        dve(lambda e, P=P: e.tensor_tensor(out=fw2k[2][0:P, :], in0=src_f32,
                                           in1=pBig[0:P, :], op=ALU.mult),
            waits=[("pe", b_)])
        dve(lambda e, P=P: e.tensor_single_scalar(out=out_bf, in_=fw2k[2][0:P, :],
                                                  scalar=g_col, op=ALU.mult))
        return cnt["dve"]

    nq_done = rms_norm(nqT[0:96, :], fw2k[0][0:96, :], 96, FC[0:96, 0:1],
                       waits=[("dve", rawAB)])
    ckv_done = rms_norm(ckvT[0:32, :], fw2k[1][0:32, :], 32, FC[0:32, 1:2],
                        waits=[("dve", nq_done)])
    ckpt("p1a", [("nqT", nqT), ("ckvT", ckvT)])

    load_bf16(wdq_b[0:96, 0:512], wdq_d[0:96, 0:512], 512, 0, rows=96)
    load_bf16(wdq_b[0:96, 512:1024], wdq_d[0:96, 512:1024], 512, 1, rows=96)
    load_bf16(wdq_b[0:96, 1024:1536], wdq_d[0:96, 1024:1536], 512, 0, rows=96)
    wdq_done = cnt["dve"]
    for g in range(4):
        for p in range(3):
            for tq in range(4):
                mm(pB[tq], wdq_b[0:96, 384 * g + 128 * p:384 * g + 128 * (p + 1)],
                   nqT[0:96, 512 * tq:512 * (tq + 1)], True, True,
                   waits=([("dve", max(wdq_done, ckv_done)), ("act", tables)]
                          if (g == 0 and p == 0 and tq == 0) else
                          (at("dve") if tq == 0 else ())))
            pq = cnt["pe"]
            if p == 0:
                dve(lambda e, g=g: e.tensor_copy(out=Qg[g][0], in_=pBig[:, :]),
                    waits=[("pe", pq)])
            else:
                dst = fw2k[0] if p == 1 else fw2k[1]
                dve(lambda e, dst=dst: e.tensor_copy(out=dst, in_=pBig[:, :]),
                    waits=[("pe", pq)])
        for w0 in range(0, T, 512):
            rope_cols(Qg[g][1], Qg[g][2], fw2k[0][:, w0:w0 + 512],
                      fw2k[1][:, w0:w0 + 512], CA, SA, w0, 512)
    q_done = cnt["dve"]
    ckpt("q", [("q00", Qg[0][0]), ("q01", Qg[0][1]), ("q02", Qg[0][2]),
               ("q30", Qg[3][0]), ("q31", Qg[3][1]), ("q32", Qg[3][2])])

    for k in range(8):
        load_bf16(wsB_b[k][:, 0:68], w_sb[128 * k:128 * (k + 1), :], 68, k % 2)
    wsb = cnt["dve"]
    for tq in range(4):
        for k in range(8):
            mm(pB[tq][0:64, :], wsB_b[k][:, 0:64], xTb[k][:, 512 * tq:512 * (tq + 1)],
               k == 0, k == 7, waits=[("dve", wsb)] if (tq == 0 and k == 0) else ())
    peB = cnt["pe"]
    act(lambda e: e.activation(out=fw2k[0][0:32, :], in_=pBig[0:32, :], func=AF.Copy,
                               scale=1.0 / 16.0), waits=[("pe", peB), ("dve", q_done)])
    act(lambda e: e.activation(out=fw2k[1][0:32, :], in_=pBig[32:64, :], func=AF.Copy,
                               scale=1.0 / 16.0))
    kr_ev = cnt["act"]
    for w0 in range(0, T, 512):
        rope_cols(krT[0:32, :], kr2tmp[0:32, :], fw2k[0][0:32, w0:w0 + 512],
                  fw2k[1][0:32, w0:w0 + 512], CA, SA, w0, 512,
                  waits=[("act", kr_ev)] if w0 == 0 else ())
    kr_done = cnt["dve"]
    dwi = dma(FC[:, 32:33], wimp_d[0:128, :], waits=at("dve"))
    for k in range(1, 8):
        dwi = dma(FC[:, 32 + k:33 + k], wimp_d[128 * k:128 * (k + 1), :])
    last_mm = [0]
    for k in range(8):
        dx = dma(fw2k[k % 2][:, :], xT_d[128 * k:128 * (k + 1), :],
                 waits=[("pe", last_mm[0]), ("dve", kr_done), ("act", kr_ev),
                        ("dma", dwi)])
        for tq in range(4):
            mm(pB[tq][0:1, :], FC[:, 32 + k:33 + k],
               fw2k[k % 2][:, 512 * tq:512 * (tq + 1)], k == 0, k == 7,
               waits=[("dma", dx)] if tq == 0 else ())
        if k >= 1:
            last_mm[0] = cnt["pe"] - 4
    peI = cnt["pe"]
    dve(lambda e: e.tensor_copy(out=ROW_A, in_=pBig[0:1, :]), waits=[("pe", peI)])
    scores_done = cnt["dve"]
    for tq in range(4):
        for k in range(8):
            mm(pS[1][0:3, :], wsB_b[k][:, 64:67], xTb[k][:, 512 * tq:512 * (tq + 1)],
               k == 0, k == 7,
               waits=[("dve", scores_done)] if (tq == 0 and k == 0) else
               (at("dve") if k == 0 else ()))
        peG = cnt["pe"]
        dve(lambda e, tq=tq: e.tensor_reduce(out=FC[0:3, 3 + tq:4 + tq],
                                             in_=pS[1][0:3, :], axis=AX.X, op=ALU.add),
            waits=[("pe", peG)])
    dve(lambda e: e.tensor_add(out=FC[0:3, 3:4], in0=FC[0:3, 3:4], in1=FC[0:3, 4:5]))
    dve(lambda e: e.tensor_add(out=FC[0:3, 5:6], in0=FC[0:3, 5:6], in1=FC[0:3, 6:7]))
    dve(lambda e: e.tensor_add(out=FC[0:3, 2:3], in0=FC[0:3, 3:4], in1=FC[0:3, 5:6]))
    dve(lambda e: e.tensor_scalar_mul(out=FC[0:3, 2:3], in0=FC[0:3, 2:3],
                                      scalar1=1.0 / T))
    gd = gdma(ROW_G, FC[0:3, 2:3], waits=at("dve"))
    dve(lambda e: e.tensor_reduce(out=ROW_S1[:, 0:1], in_=ROW_G, axis=AX.X, op=ALU.max),
        waits=[("gdma", gd)])
    dve(lambda e: e.tensor_single_scalar(out=ROW_G, in_=ROW_G,
                                         scalar=ROW_S1[:, 0:1], op=ALU.subtract),
        waits=at("dve"))
    act(lambda e: e.activation(out=ROW_G, in_=ROW_G, func=AF.Exp, scale=1.0),
        waits=at("dve"))
    dve(lambda e: e.tensor_reduce(out=ROW_S1[:, 0:1], in_=ROW_G, axis=AX.X, op=ALU.add),
        waits=at("act"))
    dve(lambda e: e.reciprocal(out=ROW_S1[:, 0:1], in_=ROW_S1[:, 0:1]))
    dve(lambda e: e.tensor_single_scalar(out=ROW_G, in_=ROW_G,
                                         scalar=ROW_S1[:, 0:1], op=ALU.mult),
        waits=at("dve"))
    gate_done = cnt["dve"]
    gmove = gdma(F32r[:, 1024:1027], ROW_G, waits=[("dve", gate_done)])
    ckpt("p1c", [("krT", krT), ("gate", ROW_G), ("scores", ROW_A),
                 ("gfc", FC[0:3, 0:8]), ("grow", F0[:, 4340:4368]), ("fc40", FC[:, 40:48])])

    for k in range(8):
        load_bf16(wink_b[k][:, 0:768], wwink_d[128 * k:128 * (k + 1), 0:768], 768, k % 2)
        load_bf16(wink_b[k][:, 768:1536], wwink_d[128 * k:128 * (k + 1), 768:1536],
                  768, k % 2)
    wink_done = cnt["dve"]
    ckpt("kwl", [("w0", wink_b[0]), ("w7", wink_b[7])])
    kwst = [BA[:, 2048 + 512 * j:2048 + 512 * (j + 1)] for j in range(4)]
    kwst_free = [0, 0, 0, 0]
    for g in range(4):
        for tq in range(4):
            for k in range(8):
                mm(pB[tq], wink_b[k][:, 384 * g:384 * g + 128],
                   xTb[k][:, 512 * tq:512 * (tq + 1)], k == 0, k == 7,
                   waits=([("dve", max(wink_done, scores_done)), ("gdma", gmove)]
                          if (g == 0 and tq == 0 and k == 0) else
                          (at("dve") if (tq == 0 and k == 0) else ())))
        pk = cnt["pe"]
        for tq in range(4):
            dve(lambda e, tq=tq: e.tensor_copy(out=kwst[tq], in_=pB[tq]),
                waits=[("pe", pk), ("gdma", kwst_free[tq])])
            kwst_free[tq] = gdma(scr_kw[3 * g + 0][:, 512 * tq:512 * (tq + 1)], kwst[tq],
                                 waits=at("dve"))
        if g == 0:
            ckpt("kwn", [("kwn0", kwst[0]), ("kwn3", kwst[3])])
        for p in (1, 2):
            for tq in range(4):
                for k in range(8):
                    mm(pB[tq], wink_b[k][:, 384 * g + 128 * p:384 * g + 128 * (p + 1)],
                       xTb[k][:, 512 * tq:512 * (tq + 1)], k == 0, k == 7,
                       waits=at("dve") if (tq == 0 and k == 0) else ())
            pk = cnt["pe"]
            dst = fw2k[0] if p == 1 else fw2k[1]
            dve(lambda e, dst=dst: e.tensor_copy(out=dst, in_=pBig[:, :]),
                waits=[("pe", pk)])
        for tq in range(4):
            w0 = 512 * tq
            for which in range(2):
                dvw = [("gdma", kwst_free[2 * (tq % 2) + which])]
                a, b = fwv[8], fwv[9]
                st = kwst[2 * (tq % 2) + which]
                if which == 0:
                    dve(lambda e, w0=w0: e.tensor_tensor(out=fwv[8][:, 0:512],
                        in0=fw2k[0][:, w0:w0 + 512], in1=CA[:, w0:w0 + 512], op=ALU.mult))
                    dve(lambda e, w0=w0: e.tensor_tensor(out=fwv[9][:, 0:512],
                        in0=fw2k[1][:, w0:w0 + 512], in1=SA[:, w0:w0 + 512], op=ALU.mult))
                    dve(lambda e, st=st: e.tensor_tensor(out=st, in0=fwv[8][:, 0:512],
                        in1=fwv[9][:, 0:512], op=ALU.subtract), waits=dvw)
                else:
                    dve(lambda e, w0=w0: e.tensor_tensor(out=fwv[8][:, 0:512],
                        in0=fw2k[0][:, w0:w0 + 512], in1=SA[:, w0:w0 + 512], op=ALU.mult))
                    dve(lambda e, w0=w0: e.tensor_tensor(out=fwv[9][:, 0:512],
                        in0=fw2k[1][:, w0:w0 + 512], in1=CA[:, w0:w0 + 512], op=ALU.mult))
                    dve(lambda e, st=st: e.tensor_tensor(out=st, in0=fwv[8][:, 0:512],
                        in1=fwv[9][:, 0:512], op=ALU.add), waits=dvw)
                kwst_free[2 * (tq % 2) + which] = gdma(
                    scr_kw[3 * g + 1 + which][:, w0:w0 + 512], st, waits=at("dve"))
    kw_done = cnt["gdma"]
    ckpt("kw", [("kw0", kwst[0]), ("kw1", kwst[1])])

    scr_vw = nc.dram_tensor("scr_vw", [16, 128, 528], BF16, kind="Internal").ap()
    for k in range(8):
        load_bf16(winv_b[k], wwinv_d[128 * k:128 * (k + 1), :], 512, k % 2)
    wvw = cnt["dve"]
    vwst = BA[:, 4096:4624]
    vwst_free = 0
    for j in range(16):
        for k in range(8):
            mm(pS[j % 2], xTb[k][:, 128 * j:128 * (j + 1)], winv_b[k], k == 0, k == 7,
               waits=[("dve", wvw)] if (j == 0 and k == 0) else
                     (at("dve") if k == 0 else ()))
        pv = cnt["pe"]
        dve(lambda e: e.memset(vwst[:, :], 0.0), waits=[("gdma", vwst_free)])
        dve(lambda e: e.memset(
            vwst.rearrange("p (h c) -> p h c", c=33)[:, :, 32:33], 1.0))
        dve(lambda e, j=j: e.tensor_copy(
            out=vwst.rearrange("p (h c) -> p h c", c=33)[:, :, 0:32],
            in_=pS[j % 2].rearrange("p (h c) -> p h c", c=32)), waits=[("pe", pv)])
        vwst_free = gdma(scr_vw[j], vwst[:, :], waits=at("dve"))
    vw_done = cnt["gdma"]
    ckpt("vw", [("vwst", vwst)])

    bs_ = bcast_row([(pB[i], 512 * i) for i in range(4)], ROW_A,
                    waits=[("dve", scores_done), ("gdma", vw_done)])
    dve(lambda e: e.tensor_copy(out=fw2k[2], in_=pBig[:, :]), waits=[("pe", bs_)])
    dve(lambda e: e.tensor_copy(out=fw2k[0], in_=fw2k[2]))
    for i in range(32):
        dve(lambda e: e.max(out=FC[:, 24:32], in_=fw2k[0]))
        dve(lambda e: e.tensor_single_scalar(out=fw2k[1], in_=fw2k[0],
                                             scalar=FC[:, 31:32], op=ALU.is_ge),
            waits=at("dve"))
        dve(lambda e: e.tensor_scalar_mul(out=fw2k[1], in0=fw2k[1], scalar1=-1e30))
        dve(lambda e: e.tensor_add(out=fw2k[0], in0=fw2k[0], in1=fw2k[1]))
    dve(lambda e: e.tensor_tensor(out=fw2k[1], in0=fw2k[2], in1=fw2k[0],
                                  op=ALU.is_gt))
    dve(lambda e: e.memset(fw2k[0], 0.0))
    dve(lambda e: e.tensor_tensor_scan(out=fw2k[2], data0=fw2k[1], data1=fw2k[0],
                                       initial=0.0, op0=ALU.add, op1=ALU.add))
    tk_rows = cnt["dve"]
    dma(sums_mc, fw2k[1][0:1, :].rearrange("one (p c) -> one p c", p=128),
        waits=[("dve", tk_rows)])
    oh_dma = dma(sums_ic, fw2k[2][0:1, :].rearrange("one (p c) -> one p c", p=128))
    for c in range(16):
        w = [("dma", oh_dma)] if c == 0 else [("pe", cnt["pe"])]
        dve(lambda e, c=c: e.tensor_single_scalar(out=fwv[c % 2][:, 0:256], in_=k_iota,
                                                  scalar=sums_ic[:, c:c + 1],
                                                  op=ALU.is_equal), waits=w)
        dve(lambda e, c=c: e.tensor_single_scalar(out=fwv[c % 2][:, 0:256],
                                                  in_=fwv[c % 2][:, 0:256],
                                                  scalar=sums_mc[:, c:c + 1],
                                                  op=ALU.mult))
        dve(lambda e, c=c: e.tensor_scalar_add(out=fwv[2 + c % 2][:, 0:1],
                                               in0=k_tcol[:, c:c + 1], scalar1=float(c)))
        mm(pS[0][0:1, 0:256], fwv[2 + c % 2][:, 0:1], fwv[c % 2][:, 0:256],
           c == 0, c == 15, waits=at("dve"))
    pidx = cnt["pe"]
    dve(lambda e: e.tensor_copy(out=ROW_B[:, 0:256], in_=pS[0][0:1, 0:256]),
        waits=[("pe", pidx)])
    dve(lambda e: e.tensor_copy(out=IIX[0:1, 0:256], in_=ROW_B[:, 0:256]),
        waits=at("dve"))
    idx_done = cnt["dve"]
    ckpt("topk", [("idx", ROW_B[:, 0:256]), ("mask", fw2k[1][0:1, :])])
    dma(IIX[:, 256:257], IIX[0:1, 0:128], waits=[("dve", idx_done)])
    dscat = dma(IIX[:, 257:258], IIX[0:1, 128:256])
    push("gpsimd", [("dma", dscat)],
         lambda e: e.indirect_dma_start(out=fw2k[0][:, 0:1024], out_offset=None,
                                        in_=x_tok,
                                        in_offset=IndirectOffsetOnAxis(IIX[:, 256:257], 0)),
         "gdma", 16)
    gath = push("gpsimd", [],
                lambda e: e.indirect_dma_start(out=fw2k[1][:, 0:1024], out_offset=None,
                                               in_=x_tok,
                                               in_offset=IndirectOffsetOnAxis(IIX[:, 257:258], 0)),
                "gdma", 16)
    for i in range(2):
        for k in range(8):
            t_ = tr(pS[1][:, 0:128], fw2k[i][:, 128 * k:128 * (k + 1)],
                    waits=([("gdma", gath)] if (i == 0 and k == 0) else []) + at("dve"))
            dve(lambda e, i=i, k=k: e.tensor_copy(out=selT[k][:, 128 * i:128 * (i + 1)],
                                                  in_=pS[1][:, 0:128]), waits=[("pe", t_)])
    sel_done = cnt["dve"]
    ckpt("sel", [("sel0", selT[0]), ("sel7", selT[7])])

    for g in range(4):
        for k in range(8):
            load_bf16(wselk_b[k][:, 0:384],
                      wselk_d[128 * k:128 * (k + 1), 384 * g:384 * (g + 1)], 384, k % 2)
        wsel = cnt["dve"]
        for p in range(3):
            for k in range(8):
                mm(pBig[:, 512 * p:512 * p + 256],
                   wselk_b[k][:, 128 * p:128 * (p + 1)], selT[k], k == 0, k == 7,
                   waits=[("dve", wsel)] if (p == 0 and k == 0) else ())
        pks = cnt["pe"]
        dve(lambda e, g=g: e.tensor_copy(out=ksall[3 * g + 0], in_=pBig[:, 0:256]),
            waits=[("pe", pks)])
        dve(lambda e: e.tensor_copy(out=fwv[4][:, 0:256], in_=pBig[:, 512:768]))
        dve(lambda e: e.tensor_copy(out=fwv[5][:, 0:256], in_=pBig[:, 1024:1280]))
        rope_cols(ksall[3 * g + 1], ksall[3 * g + 2], fwv[4][:, 0:256], fwv[5][:, 0:256],
                  CA, SA, 0, 256)
    ks_done = cnt["dve"]
    for k in range(8):
        load_bf16(wselv_b[k], wselv_d[128 * k:128 * (k + 1), :], 512, k % 2)
    wsv = cnt["dve"]
    dve(lambda e: e.memset(vsl[0][:, :], 0.0))
    dve(lambda e: e.memset(vsl[1][:, :], 0.0))
    dve(lambda e: e.memset(vsl[0].rearrange("p (h c) -> p h c", c=33)[:, :, 32:33], 1.0))
    dve(lambda e: e.memset(vsl[1].rearrange("p (h c) -> p h c", c=33)[:, :, 32:33], 1.0))
    vsl_init = cnt["dve"]
    for i in range(2):
        for k in range(8):
            mm(pS[i], selT[k][:, 128 * i:128 * (i + 1)], wselv_b[k], k == 0, k == 7,
               waits=[("dve", max(wsv, vsl_init))] if (i == 0 and k == 0) else ())
        pvs = cnt["pe"]
        dve(lambda e, i=i: e.tensor_copy(
            out=vsl[i].rearrange("p (h c) -> p h c", c=33)[:, :, 0:32],
            in_=pS[i].rearrange("p (h c) -> p h c", c=32)), waits=[("pe", pvs)])
    vs_done = cnt["dve"]
    mask_dma = dma(masks, c_masks, waits=[("dve", max(ks_done, vs_done))])
    ckpt("p2", [("ks0", ksall[0]), ("ks1", ksall[1]), ("ks2", ksall[2]),
                ("vsl0", vsl[0]), ("vsl1", vsl[1])])

    def bound(kparts, kP, kW, g, branch, qparts):
        col = 24 + (branch * 4 + g) % 8
        for ip, qp in enumerate(qparts):
            dve(lambda e, qp=qp, ip=ip: e.tensor_reduce(
                out=fwv[10][:, ip:ip + 1], in_=qp, axis=AX.X, op=ALU.max,
                apply_absolute_value=True))
        dve(lambda e: e.tensor_tensor(out=fwv[10][:, 0:1], in0=fwv[10][:, 0:1],
                                      in1=fwv[10][:, 1:2], op=ALU.max))
        dve(lambda e: e.tensor_tensor(out=fwv[10][:, 0:1], in0=fwv[10][:, 0:1],
                                      in1=fwv[10][:, 2:3], op=ALU.max))
        qm = cnt["dve"]
        t_ = tr(pS[1][0:1, 0:128], fwv[10][:, 0:1], waits=[("dve", qm)] + at("act"))
        dve(lambda e: e.tensor_reduce(out=ROW_S1[:, 1:2], in_=pS[1][0:1, 0:128],
                                      axis=AX.X, op=ALU.max), waits=[("pe", t_)])
        for ip, (kp, nr) in enumerate(kparts):
            dve(lambda e, kp=kp, nr=nr, ip=ip: e.tensor_reduce(
                out=fwv[10][0:nr, 3 + ip:4 + ip], in_=kp, axis=AX.X, op=ALU.max,
                apply_absolute_value=True))
        if True:
            dve(lambda e: e.tensor_tensor(out=fwv[10][:, 3:4], in0=fwv[10][:, 3:4],
                                          in1=fwv[10][:, 4:5], op=ALU.max))
            dve(lambda e: e.tensor_tensor(out=fwv[10][:, 3:4], in0=fwv[10][:, 3:4],
                                          in1=fwv[10][:, 5:6], op=ALU.max))
            km = cnt["dve"]
            t2 = tr(pS[1][0:1, 0:128], fwv[10][:, 3:4], waits=[("dve", km)] + at("act"))
            dve(lambda e: e.tensor_reduce(out=ROW_S1[:, 2:3], in_=pS[1][0:1, 0:128],
                                          axis=AX.X, op=ALU.max), waits=[("pe", t2)])
        else:
            raise AssertionError("unused")
        dve(lambda e: e.tensor_tensor(out=ROW_S1[:, 3:4], in0=ROW_S1[:, 1:2],
                                      in1=ROW_S1[:, 2:3], op=ALU.mult), waits=at("dve"))
        dve(lambda e: e.tensor_scalar_mul(out=ROW_S1[:, 3:4], in0=ROW_S1[:, 3:4],
                                          scalar1=-96.0 * SCALE))
        ub = cnt["dve"]
        m_ = mm(pS[1][:, 0:1], k_ones[0:1, :], ROW_S1[:, 3:4], True, True,
                waits=[("dve", ub)])
        dve(lambda e, col=col: e.tensor_copy(out=FC[:, col:col + 1], in_=pS[1][:, 0:1]),
            waits=[("pe", m_)])
        return FC[:, col:col + 1], cnt["dve"]

    def sweep(branch, g, kparts, kcolw, nkt_of_qc, vget, bexp, bexp_done, causal,
              entry_waits):
        last_exp = [0, 0]
        free_d = [cnt["dve"], cnt["dve"]]
        free_g = [0, 0]
        first = True
        for qc in range(4):
            ktmax = nkt_of_qc(qc)
            for pr in range(2):
                for kt in range(ktmax):
                    for jj in range(2):
                        hl = 2 * pr + jj
                        b = 32 * hl
                        h = 4 * g + hl
                        kc = slice(kcolw * kt, kcolw * kt + 128)
                        qs = slice(512 * qc, 512 * (qc + 1))
                        w = [("act", last_exp[jj])]
                        if first:
                            w += entry_waits + [("dve", bexp_done)]
                            first = False
                        mm(pS[jj], kparts[0][b:b + 32, kc], Qg[g][0][b:b + 32, qs],
                           True, False, tp=(b, 0), waits=w)
                        mm(pS[jj], kparts[1][b:b + 32, kc], Qg[g][1][b:b + 32, qs],
                           False, False, tp=(b, 0))
                        qk = mm(pS[jj], kparts[2][b:b + 32, kc], Qg[g][2][b:b + 32, qs],
                                False, True, tp=(b, 0))
                        push("scalar", [("pe", qk)],
                             lambda e, jj=jj, bexp=bexp: e.activation(
                                 out=PT[jj], in_=pS[jj], func=AF.Exp, bias=bexp,
                                 scale=SCALE), "act", 1)
                        last_exp[jj] = cnt["act"]
                        pdep = ("act", last_exp[jj])
                        if causal and kt >= 4 * qc:
                            i = kt - 4 * qc
                            dve(lambda e, jj=jj, i=i: e.tensor_tensor(
                                out=PT[jj], in0=PT[jj], in1=masks[:, 512 * i:512 * (i + 1)],
                                op=ALU.mult), waits=[pdep])
                            pdep = ("dve", cnt["dve"])
                        avw = [pdep]
                        if kt == 0:
                            avw += [("dve", free_d[jj]), ("gdma", free_g[jj])]
                        mm(pA[jj][0:33, :], vget(kt)[:, 33 * h:33 * h + 33], PT[jj],
                           kt == 0, kt == ktmax - 1, waits=avw)
                for jj in range(2):
                    hl = 2 * pr + jj
                    b = 32 * hl
                    avp = cnt["pe"]
                    frow = F32r[:, 512 * jj:512 * jj + 512]
                    dve(lambda e, frow=frow, jj=jj: e.tensor_copy(
                        out=frow, in_=pA[jj][32:33, :]), waits=[("pe", avp)])
                    sg_ = cnt["gdma"]
                    dve(lambda e, frow=frow: e.reciprocal(out=frow, in_=frow))
                    dve(lambda e, frow=frow, branch=branch: e.tensor_single_scalar(
                        out=frow, in_=frow,
                        scalar=F32r[:, 1024 + branch:1025 + branch], op=ALU.mult),
                        waits=at("dve") + [("gdma", gmove)])
                    fb = cnt["dve"]
                    m_ = mm(pB[jj][b:b + 32, :], k_ones[32:33, 0:32], frow,
                            True, True, tp=(32, b), waits=[("dve", fb)])
                    dve(lambda e, jj=jj: e.tensor_copy(out=fwv[4 + jj][0:32, :],
                                                       in_=pA[jj][0:32, :]))
                    free_d[jj] = cnt["dve"]
                    free_g[jj] = sg_
                    mv = gdma(fwv[6 + jj][b:b + 32, :], fwv[4 + jj][0:32, :],
                              waits=at("dve"))
                    qs = slice(512 * qc, 512 * (qc + 1))
                    if branch == 0:
                        dve(lambda e, g=g, b=b, jj=jj, qs=qs: e.tensor_tensor(
                            out=agg[g][b:b + 32, qs], in0=fwv[6 + jj][b:b + 32, :],
                            in1=pB[jj][b:b + 32, :], op=ALU.mult),
                            waits=[("gdma", mv), ("pe", m_)])
                    else:
                        dve(lambda e, b=b, jj=jj: e.tensor_tensor(
                            out=fwv[8 + jj][b:b + 32, :], in0=fwv[6 + jj][b:b + 32, :],
                            in1=pB[jj][b:b + 32, :], op=ALU.mult),
                            waits=[("gdma", mv), ("pe", m_)])
                        dve(lambda e, g=g, b=b, jj=jj, qs=qs: e.tensor_tensor(
                            out=agg[g][b:b + 32, qs], in0=agg[g][b:b + 32, qs],
                            in1=fwv[8 + jj][b:b + 32, :], op=ALU.add))
        return cnt["gp"]

    a1_pe_gate = cnt["pe"]
    for si, (dst, src) in enumerate(((wdk_b, wdk_d), (wdv_b, wdv_d))):
        dk = dma(fwv[si][:, 0:512][0:32, :], src, waits=[("gdma", kw_done)])
        dve(lambda e, dst=dst, si=si: e.tensor_copy(out=dst[0:32, :],
                                                    in_=fwv[si][0:32, 0:512]),
            waits=[("dma", dk)])
    wdkv = cnt["dve"]
    for hl in range(4):
        gdma(krx1[32 * hl:32 * hl + 32, :], krT[0:32, :], waits=[("dve", kr_done)])
        gdma(krx2[32 * hl:32 * hl + 32, :], kr2tmp[0:32, :])
    krx_done = cnt["gdma"]
    for j in range(16):
        dve(lambda e, j=j: e.memset(vslots[j][:, :], 0.0),
            waits=[("gdma", max(vw_done, gath))] if j == 0 else ())
        dve(lambda e, j=j: e.memset(
            vslots[j].rearrange("p (h c) -> p h c", c=33)[:, :, 32:33], 1.0))
        vz = cnt["dve"]
        pv = mm(pS[j % 2], ckvT[0:32, 128 * j:128 * (j + 1)], wdv_b[0:32, :],
                True, True, waits=[("dve", max(wdkv, vz, ckv_done))])
        dve(lambda e, j=j: e.tensor_copy(
            out=vslots[j].rearrange("p (h c) -> p h c", c=33)[:, :, 0:32],
            in_=pS[j % 2].rearrange("p (h c) -> p h c", c=32)), waits=[("pe", pv)])
    v1_done = cnt["dve"]

    for g in range(4):
        for tq in range(4):
            mm(pB[tq], wdk_b[0:32, 128 * g:128 * (g + 1)],
               ckvT[0:32, 512 * tq:512 * (tq + 1)], True, True,
               waits=[("dve", v1_done), ("act", cnt["act"])] if tq == 0 else ())
        pk = cnt["pe"]
        dve(lambda e: e.tensor_copy(out=kn_g[:, :], in_=pBig[:, :]), waits=[("pe", pk)])
        kn_done = cnt["dve"]
        bexp, bdone = bound([(kn_g, 128), (krx1, 128), (krx2, 128)], 128, 2048, g, 0,
                            Qg[g])
        sweep(0, g, [kn_g, krx1, krx2], 128, lambda qc: 4 * qc + 4,
              lambda kt: vslots[kt], bexp, bdone, True,
              [("dve", kn_done), ("gdma", krx_done), ("dma", mask_dma)])

    ckpt("a1", [("agg0", agg[0]), ("agg3", agg[3])])

    for g in range(4):
        kp = [ksall[3 * g + 0], ksall[3 * g + 1], ksall[3 * g + 2]]
        bexp, bdone = bound([(k, 128) for k in kp], 128, 256, g, 1, Qg[g])
        sweep(1, g, kp, 128, lambda qc: 2, lambda kt: vsl[kt], bexp, bdone, False,
              [("dve", vs_done)])

    ckpt("a2", [("agg0", agg[0]), ("agg3", agg[3])])

    for j in range(16):
        gdma(vslots[j][:, :], scr_vw[j], waits=[("gp", cnt["gp"])] if j == 0 else ())
    vw_re = cnt["gdma"]
    for g in range(4):
        kwdma = 0
        for p in range(3):
            kwdma = dma(kwbuf[0][p], scr_kw[3 * g + p],
                        waits=[("act", cnt["act"])] if p == 0 else ())
        if g == 0:
            ckpt("a3pre", [("kwb0", kwbuf[0][0]), ("kwb1", kwbuf[0][1]),
                           ("kwb2", kwbuf[0][2]), ("vs0", vslots[0])])
        kp = kwbuf[0]
        bexp, bdone = bound([(k, 128) for k in kp], 128, 2048, g, 2, Qg[g])
        sweep(2, g, kp, 128, lambda qc: 4 * qc + 4, lambda kt: vslots[kt], bexp, bdone,
              True, [("dma", kwdma), ("gdma", vw_re)])
        if g == 0:
            ckpt("a3g0", [("agg0", agg[0])])
    att_done_gp = cnt["gp"]
    ckpt("a3", [("agg0", agg[0]), ("agg1", agg[1]), ("agg2", agg[2]),
                ("agg3", agg[3])])

    for kx in range(4):
        load_bf16(wproj_b[kx], wproj_d[128 * kx:128 * (kx + 1), :], 1024, kx % 2)
    wpj = cnt["dve"]
    zq = [FW[:, 512 * i:512 * (i + 1)] for i in range(8)]
    scr_dma = [0] * 8
    for qc in range(4):
        qs = slice(512 * qc, 512 * (qc + 1))
        for mo in range(8):
            for kx in range(4):
                mm(pS[mo % 2], wproj_b[kx][:, 128 * mo:128 * (mo + 1)], agg[kx][:, qs],
                   kx == 0, kx == 3,
                   waits=[("dve", wpj), ("gp", att_done_gp)] if (qc == 0 and mo == 0 and kx == 0)
                   else ([("dve", cnt["dve"])] if kx == 0 else ()))
            pp = cnt["pe"]
            dx = dma(fwv[8 + mo % 2][:, :], xT_d[128 * mo:128 * (mo + 1), qs],
                     waits=at("dve"))
            dve(lambda e, mo=mo: e.tensor_add(out=zq[mo], in0=fwv[8 + mo % 2][:, :],
                                              in1=pS[mo % 2]),
                waits=[("pe", pp), ("dma", dx), ("dma", scr_dma[mo])])
        z_done = cnt["dve"]
        for mo in range(8):
            act(lambda e, mo=mo: e.activation(out=fwv[10][:, :], in_=zq[mo],
                                              func=AF.Square, scale=1.0),
                waits=[("dve", z_done)] if mo == 0 else at("pe"))
            sq = cnt["act"]
            mm(pA[0][0:1, :], k_onescol, zq[mo], mo == 0, mo == 7,
               waits=[("act", sq)])
            mm(pA[1][0:1, :], k_onescol, fwv[10][:, :], mo == 0, mo == 7)
        pst = cnt["pe"]
        dve(lambda e: e.tensor_scalar_mul(out=ROW_MEAN, in0=pA[0][0:1, :],
                                          scalar1=1.0 / C), waits=[("pe", pst)])
        dve(lambda e: e.tensor_scalar_mul(out=ROW_INV, in0=pA[1][0:1, :],
                                          scalar1=1.0 / C))
        dve(lambda e: e.tensor_tensor(out=ROW_BB, in0=ROW_MEAN, in1=ROW_MEAN,
                                      op=ALU.mult))
        dve(lambda e: e.tensor_tensor(out=ROW_INV, in0=ROW_INV, in1=ROW_BB,
                                      op=ALU.subtract))
        dve(lambda e: e.tensor_scalar_add(out=ROW_INV, in0=ROW_INV, scalar1=EPS_LN))
        act(lambda e: e.activation(out=ROW_INV, in_=ROW_INV, func=AF.Sqrt, scale=1.0),
            waits=at("dve"))
        dve(lambda e: e.reciprocal(out=ROW_INV, in_=ROW_INV), waits=at("act"))
        dve(lambda e: e.tensor_tensor(out=ROW_BB, in0=ROW_MEAN, in1=ROW_INV,
                                      op=ALU.mult))
        dve(lambda e: e.tensor_scalar_mul(out=ROW_BB, in0=ROW_BB, scalar1=-1.0))
        rows_ab = cnt["dve"]
        mm(pB[2], k_ones[0:1, :], ROW_INV, True, True, waits=[("dve", rows_ab)])
        ab_pe = mm(pB[3], k_ones[0:1, :], ROW_BB, True, True)
        for mo in range(8):
            dve(lambda e, mo=mo: e.tensor_tensor(out=fwv[10][:, :], in0=zq[mo],
                                                 in1=pB[2], op=ALU.mult),
                waits=[("pe", ab_pe)] if mo == 0 else ())
            dve(lambda e, mo=mo: e.tensor_add(out=zq[mo], in0=fwv[10][:, :], in1=pB[3]))
            scr_dma[mo] = dma(scr_x1[128 * mo:128 * (mo + 1), qs], zq[mo],
                              waits=at("dve"))
        x1_done = cnt["dve"]
        dve(lambda e: e.tensor_copy(out=fwv[10][:, :], in_=zq[0]),
            waits=[("dve", x1_done)])
        dve(lambda e: e.tensor_copy(out=fwv[11][:, :], in_=zq[0]))
        for mo in range(1, 8):
            dve(lambda e, mo=mo: e.tensor_tensor(out=fwv[10][:, :], in0=fwv[10][:, :],
                                                 in1=zq[mo], op=ALU.max))
            dve(lambda e, mo=mo: e.tensor_tensor(out=fwv[11][:, :], in0=fwv[11][:, :],
                                                 in1=zq[mo], op=ALU.min))
        dve(lambda e: e.tensor_scalar_mul(out=fwv[11][:, :], in0=fwv[11][:, :],
                                          scalar1=-1.0))
        dve(lambda e: e.tensor_tensor(out=fwv[10][:, :], in0=fwv[10][:, :],
                                      in1=fwv[11][:, :], op=ALU.max))
        am = cnt["dve"]
        for j in range(4):
            t1 = tr(pA[0][:, 0:128], fwv[10][:, 128 * j:128 * (j + 1)],
                    waits=[("dve", am)])
            dve(lambda e, j=j: e.tensor_reduce(out=fwv[11][:, j:j + 1],
                                               in_=pA[0][:, 0:128], axis=AX.X,
                                               op=ALU.max), waits=[("pe", t1)])
            t2 = tr(pA[1][0:1, 0:128], fwv[11][:, j:j + 1], waits=at("dve"))
            dve(lambda e, j=j: e.tensor_copy(out=ROW_SC[:, 128 * j:128 * (j + 1)],
                                             in_=pA[1][0:1, 0:128]), waits=[("pe", t2)])
        dve(lambda e: e.tensor_scalar_max(out=ROW_SC, in0=ROW_SC, scalar1=1e-5))
        dve(lambda e, qs=qs: e.tensor_copy(out=ROW_RX[:, qs], in_=ROW_SC))
        dve(lambda e: e.reciprocal(out=ROW_SC, in_=ROW_SC), waits=at("dve"))
        dve(lambda e: e.tensor_scalar_mul(out=ROW_SC, in0=ROW_SC, scalar1=127.0))
        srow = cnt["dve"]
        sq_pe = mm(pB[2], k_ones[0:1, :], ROW_SC, True, True, waits=[("dve", srow)])
        for mo in range(8):
            quant_round(xq[mo][:, qs], zq[mo], pB[2], fwv[10][:, :], fwv[11][:, :],
                        waits=[("pe", sq_pe)] if mo == 0 else ())
    p4_done = cnt["act"]
    ckpt("p4", [("xq0", xq[0]), ("xq7", xq[7]), ("rx", ROW_RX)])

    def wmean(src_d, nchunk, width, dst_row):
        dve(lambda e: e.memset(FC[:, 20:21], 0.0))
        lastp = [cnt["dve"]]
        for ck in range(nchunk):
            dk = dma(fw2k[ck % 2][:, 0:width], src_d[ck], waits=[("dve", lastp[0])])
            dve(lambda e, ck=ck: e.tensor_reduce(out=FC[:, 21:22],
                                                 in_=fw2k[ck % 2][:, 0:width],
                                                 axis=AX.X, op=ALU.add,
                                                 apply_absolute_value=True),
                waits=[("dma", dk)])
            dve(lambda e: e.tensor_add(out=FC[:, 20:21], in0=FC[:, 20:21],
                                       in1=FC[:, 21:22]))
            lastp[0] = cnt["dve"]
        t_ = tr(pA[0][0:1, 0:128], FC[:, 20:21], waits=[("dve", lastp[0]), ("act", cnt["act"])])
        dve(lambda e: e.tensor_reduce(out=dst_row, in_=pA[0][0:1, 0:128], axis=AX.X,
                                      op=ALU.add), waits=[("pe", t_)])
        dve(lambda e, dst_row=dst_row: e.tensor_scalar_mul(out=dst_row, in0=dst_row,
                                                           scalar1=1.0 / (DH * C)))
        dve(lambda e, dst_row=dst_row: e.tensor_scalar_max(out=dst_row, in0=dst_row,
                                                           scalar1=1e-5))
        return cnt["dve"]

    w1halves = [w1T_d[128 * (hk // 2):128 * (hk // 2) + 128,
                      2048 * (hk % 2):2048 * (hk % 2) + 2048] for hk in range(16)]
    cm1 = wmean(w1halves, 16, 2048, ROW_CM[:, 0:1])
    w2chunks = [w2T_d[128 * ck:128 * (ck + 1), :] for ck in range(32)]
    cm2 = wmean(w2chunks, 32, 1024, ROW_CM[:, 1:2])
    dve(lambda e: e.reciprocal(out=ROW_CM[:, 2:3], in_=ROW_CM[:, 0:1]),
        waits=[("dve", cm1)])
    dve(lambda e: e.reciprocal(out=ROW_CM[:, 3:4], in_=ROW_CM[:, 1:2]),
        waits=[("dve", cm2)])
    scl = cnt["dve"]
    m_ = mm(pS[0][:, 0:1], k_ones[0:1, :], ROW_CM[:, 2:3], True, True,
            waits=[("dve", scl)])
    dve(lambda e: e.tensor_copy(out=FC[:, 16:17], in_=pS[0][:, 0:1]), waits=[("pe", m_)])
    m_ = mm(pS[0][:, 0:1], k_ones[0:1, :], ROW_CM[:, 3:4], True, True, waits=at("dve"))
    dve(lambda e: e.tensor_copy(out=FC[:, 17:18], in_=pS[0][:, 0:1]), waits=[("pe", m_)])
    swcols = cnt["dve"]
    dve(lambda e: e.tensor_single_scalar(out=ROW_RX, in_=ROW_RX,
                                         scalar=ROW_CM[:, 0:1], op=ALU.mult),
        waits=[("dve", swcols), ("act", p4_done)])
    dve(lambda e: e.tensor_scalar_mul(out=ROW_RX, in0=ROW_RX, scalar1=1.0 / 127.0))
    rx_done = cnt["dve"]

    def wquant(src_getter, nchunk, width, scol, out_write):
        for ck in range(nchunk):
            hp = ck % 2
            dk = dma(fw2k[hp][:, 0:width], src_getter(ck),
                     waits=[("act", cnt["act"])])
            dve(lambda e, hp=hp, width=width, scol=scol: e.tensor_single_scalar(
                out=fw2k[2][:, 0:width], in_=fw2k[hp][:, 0:width], scalar=scol,
                op=ALU.mult), waits=[("dma", dk)])
            dve(lambda e, width=width: e.tensor_scalar_min(
                out=fw2k[2][:, 0:width], in0=fw2k[2][:, 0:width], scalar1=1.0))
            dve(lambda e, width=width: e.tensor_scalar_max(
                out=fw2k[2][:, 0:width], in0=fw2k[2][:, 0:width], scalar1=-1.0))
            act(lambda e, hp=hp, width=width: e.activation(
                out=fw2k[hp][:, 0:width], in_=fw2k[2][:, 0:width], func=AF.Identity,
                bias=k_magic, scale=1.0), waits=at("dve"))
            dve(lambda e, hp=hp, width=width: e.tensor_copy(
                out=fw2k[2].bitcast(I32)[:, 0:width],
                in_=fw2k[hp].bitcast(I32)[:, 0:width]), waits=at("act"))
            act(lambda e, ck=ck, width=width: e.activation(
                out=out_write(ck), in_=fw2k[2][:, 0:width], func=AF.Identity,
                bias=k_negmagic, scale=1.0), waits=at("dve"))

    wquant(lambda hk: w1halves[hk], 16, 2048, FC[:, 16:17],
           lambda hk: w1q[hk // 2][:, 2048 * (hk % 2):2048 * (hk % 2) + 2048])
    w2st_free = [0, 0]
    def w2_write(ck):
        return w2bf[ck % 2]
    for ck in range(32):
        hp = ck % 2
        dk = dma(fw2k[hp][:, 0:1024], w2chunks[ck], waits=[("act", cnt["act"])])
        dve(lambda e, hp=hp: e.tensor_single_scalar(
            out=fw2k[2][:, 0:1024], in_=fw2k[hp][:, 0:1024], scalar=FC[:, 17:18],
            op=ALU.mult), waits=[("dma", dk)])
        dve(lambda e: e.tensor_scalar_min(out=fw2k[2][:, 0:1024],
                                          in0=fw2k[2][:, 0:1024], scalar1=1.0))
        dve(lambda e: e.tensor_scalar_max(out=fw2k[2][:, 0:1024],
                                          in0=fw2k[2][:, 0:1024], scalar1=-1.0))
        act(lambda e, hp=hp: e.activation(out=fw2k[hp][:, 0:1024],
                                          in_=fw2k[2][:, 0:1024], func=AF.Identity,
                                          bias=k_magic, scale=1.0), waits=at("dve"))
        dve(lambda e, hp=hp: e.tensor_copy(out=fw2k[2].bitcast(I32)[:, 0:1024],
                                           in_=fw2k[hp].bitcast(I32)[:, 0:1024]),
            waits=at("act"))
        act(lambda e, ck=ck: e.activation(out=w2bf[ck % 2], in_=fw2k[2][:, 0:1024],
                                          func=AF.Identity, bias=k_negmagic, scale=1.0),
            waits=at("dve") + [("gdma", w2st_free[ck % 2])])
        w2st_free[ck % 2] = gdma(scr_w2[128 * ck:128 * (ck + 1), :], w2bf[ck % 2],
                                 waits=at("act"))
    wq_done = cnt["gdma"]
    ckpt("wq", [("w1q0", w1q[0]), ("w1q7", w1q[7]), ("cm", ROW_CM[:, 0:8]),
                ("sw", FC[:, 16:18])])

    final_dma = 0
    for s in range(4):
        L = 512 * s
        left = max(0, L - 1)
        right = min(T, L + 513)
        Wd = right - left
        of0 = 1 if s == 0 else 0
        Na = min(512, Wd)
        Nb = Wd - Na
        bx = mm(pB[2], k_ones[0:1, :], ROW_RX[:, left:left + Na], True, True,
                waits=[("dve", rx_done), ("gdma", wq_done)] if s == 0 else at("dve"))
        if Nb:
            bx = mm(pB[3][:, 0:Nb], k_ones[0:1, :], ROW_RX[:, left + Na:right],
                    True, True)
        dve(lambda e, Na=Na: e.tensor_copy(out=fwv[6][:, 0:Na], in_=pB[2][:, 0:Na]),
            waits=[("pe", bx)])
        if Nb:
            dve(lambda e, Nb=Nb: e.tensor_copy(out=fwv[7][:, 0:Nb], in_=pB[3][:, 0:Nb]))
        rx_sb = cnt["dve"]
        hz = [0, 0]
        for m in range(32):
            for k in range(8):
                mm(pS[m % 2][:, 0:Na], w1q[k][:, 128 * m:128 * (m + 1)],
                   xq[k][:, left:left + Na], k == 0, k == 7,
                   waits=([("pe", bx), ("act", cnt["act"])] if (m == 0 and k == 0)
                          else ([("act", hz[m % 2])] if k == 0 else ())))
            if Nb:
                for k in range(8):
                    mm(pA[m % 2][:, 0:Nb], w1q[k][:, 128 * m:128 * (m + 1)],
                       xq[k][:, left + Na:right], k == 0, k == 7)
            pm = cnt["pe"]
            if s == 0:
                dve(lambda e: e.memset(h1f[:, 0:1], 0.0))
            if of0 + Wd < 514:
                dve(lambda e, o=of0 + Wd: e.memset(h1f[:, o:514], 0.0))
            dve(lambda e, m=m, of0=of0, Na=Na: e.tensor_tensor(
                out=h1f[:, of0:of0 + Na], in0=pS[m % 2][:, 0:Na], in1=fwv[6][:, 0:Na],
                op=ALU.mult), waits=[("pe", pm), ("dve", rx_sb)] + at("act"))
            if Nb:
                dve(lambda e, m=m, of0=of0, Na=Na, Nb=Nb: e.tensor_tensor(
                    out=h1f[:, of0 + Na:of0 + Na + Nb], in0=pA[m % 2][:, 0:Nb],
                    in1=fwv[7][:, 0:Nb], op=ALU.mult))
            hv = cnt["dve"]
            act(lambda e, m=m: e.activation(out=fwv[0][:, :], in_=h1f[:, 0:512],
                                            func=AF.Identity, scale=k_cw[:, 3 * m:3 * m + 1],
                                            bias=k_cb[:, m:m + 1]), waits=[("dve", hv)])
            act(lambda e, m=m: e.activation(out=fwv[1][:, :], in_=h1f[:, 1:513],
                                            func=AF.Identity,
                                            scale=k_cw[:, 3 * m + 1:3 * m + 2]))
            dve(lambda e: e.tensor_add(out=fwv[0][:, :], in0=fwv[0][:, :],
                                       in1=fwv[1][:, :]), waits=at("act"))
            act(lambda e, m=m: e.activation(out=fwv[1][:, :], in_=h1f[:, 2:514],
                                            func=AF.Identity,
                                            scale=k_cw[:, 3 * m + 2:3 * m + 3]),
                waits=at("dve"))
            dve(lambda e: e.tensor_add(out=fwv[0][:, :], in0=fwv[0][:, :],
                                       in1=fwv[1][:, :]), waits=at("act"))
            act(lambda e, m=m: e.activation(out=hsl[m], in_=fwv[0][:, :], func=AF.Silu,
                                            scale=1.0), waits=at("dve"))
            hz[m % 2] = cnt["act"]
            if m == 0:
                dve(lambda e: e.tensor_copy(out=fwv[2][:, :], in_=hsl[0]),
                    waits=at("act"))
                dve(lambda e: e.tensor_copy(out=fwv[8][:, :], in_=hsl[0]))
            else:
                dve(lambda e, m=m: e.tensor_tensor(out=fwv[2][:, :], in0=fwv[2][:, :],
                                                   in1=hsl[m], op=ALU.max),
                    waits=at("act"))
                dve(lambda e, m=m: e.tensor_tensor(out=fwv[8][:, :], in0=fwv[8][:, :],
                                                  in1=hsl[m], op=ALU.min))
        dve(lambda e: e.tensor_scalar_mul(out=fwv[9][:, :], in0=fwv[8][:, :],
                                          scalar1=-1.0))
        dve(lambda e: e.tensor_tensor(out=fwv[2][:, :], in0=fwv[2][:, :],
                                      in1=fwv[9][:, :], op=ALU.max))
        am = cnt["dve"]
        if s == 0:
            ckpt("ffn1", [("h0", hsl[0]), ("h5", hsl[5]), ("h31", hsl[31]),
                          ("hmax", fwv[2]), ("h1f", h1f), ("rxsb", fwv[6]),
                          ("cw", k_cw), ("cb", k_cb)])
        for j in range(4):
            t1 = tr(pA[0][:, 0:128], fwv[2][:, 128 * j:128 * (j + 1)],
                    waits=[("dve", am)])
            dve(lambda e, j=j: e.tensor_reduce(out=fwv[3][:, j:j + 1],
                                               in_=pA[0][:, 0:128], axis=AX.X,
                                               op=ALU.max), waits=[("pe", t1)])
            t2 = tr(pA[1][0:1, 0:128], fwv[3][:, j:j + 1], waits=at("dve"))
            dve(lambda e, j=j: e.tensor_copy(out=ROW_SC[:, 128 * j:128 * (j + 1)],
                                             in_=pA[1][0:1, 0:128]), waits=[("pe", t2)])
        dve(lambda e: e.tensor_scalar_max(out=ROW_SC, in0=ROW_SC, scalar1=1e-5))
        dve(lambda e: e.tensor_single_scalar(out=ROW_INV, in_=ROW_SC,
                                             scalar=ROW_CM[:, 1:2], op=ALU.mult),
            waits=at("dve"))
        dve(lambda e: e.tensor_scalar_mul(out=ROW_INV, in0=ROW_INV, scalar1=1.0 / 127.0))
        dve(lambda e: e.reciprocal(out=ROW_SC, in_=ROW_SC))
        dve(lambda e: e.tensor_scalar_mul(out=ROW_SC, in0=ROW_SC, scalar1=127.0))
        hs = cnt["dve"]
        hb = mm(pB[2], k_ones[0:1, :], ROW_SC, True, True, waits=[("dve", hs)])
        off = of0 + (L - left)
        for m in range(32):
            quant_round(hsl[m], hsl[m], pB[2], fwv[0][:, :], fwv[1][:, :],
                        waits=[("pe", hb)] if m == 0 else ())
        hq_done = cnt["act"]
        if s == 0:
            ckpt("hq", [("hq0", hsl[0]), ("hq31", hsl[31]), ("rh", ROW_INV[:, 0:512]),
                        ("hsc", ROW_SC[:, 0:512])])
        rb = mm(pS[0], k_ones[0:1, :], ROW_INV[:, 0:512], True, True,
                waits=[("act", hq_done)])
        dve(lambda e: e.tensor_copy(out=fwv[3][:, :], in_=pS[0]), waits=[("pe", rb)])
        rh_sb = cnt["dve"]
        w2dma = [0, 0]
        w2pe = [0, 0]
        stat_first = True
        for blk in range(2):
            for k in range(32):
                w2dma[k % 2] = dma(w2bf[k % 2], scr_w2[128 * k:128 * (k + 1), :],
                                   waits=[("pe", w2pe[k % 2])]
                                   + ([("gdma", wq_done)] if s == 0 and blk == 0
                                      else []))
                for moin in range(4):
                    mo = 4 * blk + moin
                    mm(pB[moin], w2bf[k % 2][:, 128 * mo:128 * (mo + 1)],
                       hsl[k][:, 0:512], k == 0, k == 31,
                       waits=[("dma", w2dma[k % 2])] if moin == 0 else ())
                w2pe[k % 2] = cnt["pe"]
            pf = cnt["pe"]
            for moin in range(4):
                mo = 4 * blk + moin
                dve(lambda e, moin=moin: e.tensor_tensor(out=fwv[4][:, :], in0=pB[moin],
                                                         in1=fwv[3][:, :], op=ALU.mult),
                    waits=[("pe", pf if moin == 0 else cnt["pe"]), ("dve", rh_sb)]
                    + at("dma"))
                dx = dma(fwv[5][:, :], scr_x1[128 * mo:128 * (mo + 1), L:L + 512],
                         waits=())
                dve(lambda e: e.tensor_add(out=fwv[4][:, :], in0=fwv[4][:, :],
                                           in1=fwv[5][:, :]), waits=[("dma", dx)])
                zd = cnt["dve"]
                dma(scr_z2[128 * mo:128 * (mo + 1), L:L + 512], fwv[4][:, :],
                    waits=[("dve", zd)])
                act(lambda e: e.activation(out=fwv[6][:, :], in_=fwv[4][:, :],
                                           func=AF.Square, scale=1.0),
                    waits=[("dve", zd)])
                sq = cnt["act"]
                mm(pA[0][0:1, :], k_onescol, fwv[4][:, :], stat_first, mo == 7,
                   waits=[("act", sq)])
                mm(pA[1][0:1, :], k_onescol, fwv[6][:, :], stat_first, mo == 7)
                stat_first = False
        pst = cnt["pe"]
        if s == 0:
            ckpt("z2", [("z2s", scr_z2[0:128, 0:512]), ("z2b", scr_z2[896:1024, 0:512]),
                        ("x1s", scr_x1[0:128, 0:512]), ("x1b", scr_x1[896:1024, 0:512]),
                        ("hq0", hsl[0]), ("hq31", hsl[31])])
        dve(lambda e: e.tensor_scalar_mul(out=ROW_MEAN, in0=pA[0][0:1, :],
                                          scalar1=1.0 / C), waits=[("pe", pst)])
        dve(lambda e: e.tensor_scalar_mul(out=ROW_INV, in0=pA[1][0:1, :],
                                          scalar1=1.0 / C))
        dve(lambda e: e.tensor_tensor(out=ROW_BB, in0=ROW_MEAN, in1=ROW_MEAN,
                                      op=ALU.mult))
        dve(lambda e: e.tensor_tensor(out=ROW_INV, in0=ROW_INV, in1=ROW_BB,
                                      op=ALU.subtract))
        dve(lambda e: e.tensor_scalar_add(out=ROW_INV, in0=ROW_INV, scalar1=EPS_LN))
        act(lambda e: e.activation(out=ROW_INV, in_=ROW_INV, func=AF.Sqrt, scale=1.0),
            waits=at("dve"))
        dve(lambda e: e.reciprocal(out=ROW_INV, in_=ROW_INV), waits=at("act"))
        dve(lambda e: e.tensor_tensor(out=ROW_BB, in0=ROW_MEAN, in1=ROW_INV,
                                      op=ALU.mult))
        dve(lambda e: e.tensor_scalar_mul(out=ROW_BB, in0=ROW_BB, scalar1=-1.0))
        rws = cnt["dve"]
        mm(pB[2], k_ones[0:1, :], ROW_INV, True, True, waits=[("dve", rws)])
        abp = mm(pB[3], k_ones[0:1, :], ROW_BB, True, True)
        for mo in range(8):
            dx = dma(fwv[5][:, :], scr_z2[128 * mo:128 * (mo + 1), L:L + 512],
                     waits=[("dve", cnt["dve"])])
            dve(lambda e: e.tensor_tensor(out=fwv[4][:, :], in0=fwv[5][:, :],
                                          in1=pB[2], op=ALU.mult),
                waits=[("pe", abp)] + at("dma"))
            dve(lambda e: e.tensor_add(out=fwv[4][:, :], in0=fwv[4][:, :], in1=pB[3]))
            final_dma = dma(out_dram[128 * mo:128 * (mo + 1), L:L + 512], fwv[4][:, :],
                            waits=at("dve"))
    push("vector", [("dma", final_dma)], lambda e: e.memset(FC[:, 63:64], 0.0), "dve", 1)

    assemble()
    return nc


def _prep_inputs(inputs):
    f32 = np.float32
    d = {}
    wq = np.ascontiguousarray(inputs["L0_attn_w_cq"], f32)
    wkv = np.ascontiguousarray(inputs["L0_attn_w_ckv"], f32)
    d["w_sa"] = np.ascontiguousarray(np.concatenate([wq, wkv], axis=1), f32)
    d["w_sb"] = np.ascontiguousarray(np.concatenate(
        [inputs["L0_attn_w_krope"], inputs["L0_attn_w_gate"], inputs["L0_attn_w_imp"]],
        axis=1), f32)
    dqn = np.asarray(inputs["L0_attn_w_dq_nope"], f32).reshape(QL, 16, 32)
    dqr = np.asarray(inputs["L0_attn_w_dq_rope"], f32).reshape(QL, 16, 64)
    parts = []
    for g in range(4):
        parts += [dqn[:, 4 * g:4 * g + 4, :].reshape(QL, 128),
                  dqr[:, 4 * g:4 * g + 4, 0:32].reshape(QL, 128),
                  dqr[:, 4 * g:4 * g + 4, 32:64].reshape(QL, 128)]
    d["wdq"] = np.ascontiguousarray(np.concatenate(parts, axis=1), f32)

    def group_k(w):
        w = np.asarray(w, f32).reshape(C, 16, 96)
        ps = []
        for g in range(4):
            ps += [w[:, 4 * g:4 * g + 4, 0:32].reshape(C, 128),
                   w[:, 4 * g:4 * g + 4, 32:64].reshape(C, 128),
                   w[:, 4 * g:4 * g + 4, 64:96].reshape(C, 128)]
        return np.ascontiguousarray(np.concatenate(ps, axis=1), f32)

    d["wwink"] = group_k(inputs["L0_attn_w_wink"])
    d["wselk"] = group_k(inputs["L0_attn_w_selk"])
    d["wdk"] = np.ascontiguousarray(inputs["L0_attn_w_dk_nope"], f32)
    d["wdv"] = np.ascontiguousarray(inputs["L0_attn_w_dv"], f32)
    d["wwinv"] = np.ascontiguousarray(inputs["L0_attn_w_winv"], f32)
    d["wselv"] = np.ascontiguousarray(inputs["L0_attn_w_selv"], f32)
    d["wproj"] = np.ascontiguousarray(inputs["L0_attn_w_proj"], f32)
    d["w1T"] = np.ascontiguousarray(np.asarray(inputs["L0_ffn_w1"], f32).T)
    d["w2T"] = np.ascontiguousarray(np.asarray(inputs["L0_ffn_w2"], f32).T)
    d["wimp"] = np.ascontiguousarray(np.asarray(inputs["L0_attn_w_imp"], f32).reshape(C, 1))
    d["gq"] = np.ascontiguousarray(np.asarray(inputs["L0_attn_g_qnorm"], f32).reshape(QL, 1))
    d["gkv"] = np.ascontiguousarray(np.asarray(inputs["L0_attn_g_kvnorm"], f32).reshape(KVL, 1))
    cw = np.asarray(inputs["L0_ffn_conv_w"], f32).reshape(32, 128, 3)
    d["cwcol"] = np.ascontiguousarray(cw.transpose(1, 0, 2).reshape(128, 96))
    d["cbcol"] = np.ascontiguousarray(np.asarray(inputs["L0_ffn_conv_b"], f32).reshape(32, 128).T)
    d["c_trow"] = np.arange(T, dtype=f32).reshape(1, T)
    d["c_drow"] = (np.arange(128) % 32).astype(f32).reshape(1, 128)
    d["c_ident"] = np.eye(128, dtype=f32)
    d["c_iota256"] = np.broadcast_to(np.arange(1, 257, dtype=f32)[None, :],
                                     (128, 256)).copy()
    d["c_tcol"] = np.broadcast_to((16.0 * np.arange(128, dtype=f32))[:, None],
                                  (128, 16)).copy()
    import ml_dtypes
    mk = np.zeros((128, 2048), np.float32)
    p = np.arange(128)[:, None]
    j = np.arange(512)[None, :]
    for i in range(4):
        mk[:, 512 * i:512 * (i + 1)] = (j >= p + 128 * i).astype(f32)
    d["c_masks"] = mk.astype(ml_dtypes.bfloat16)
    d["c_ones"] = np.ones((1, 128), f32)
    return d


def kernel(**inputs):
    if "nc" not in _CACHE:
        _CACHE["nc"] = build_program()
    nc = _CACHE["nc"]
    shared = _prep_inputs(inputs)
    x = np.asarray(inputs["L0_attn_x"], np.float32)
    in_maps = []
    for b in range(8):
        m = dict(shared)
        m["x_tok"] = np.ascontiguousarray(x[b])
        m["xT"] = np.ascontiguousarray(x[b].T)
        in_maps.append(m)
    res = run_bass_kernel_spmd(nc, in_maps, core_ids=list(range(8)))
    out = np.stack([np.asarray(r["outT"]).T for r in res.results], axis=0)
    return out.astype(np.float32)


# version 10 (final)
# speedup vs baseline: 1.0023x; 1.0023x over previous
import math
import numpy as np
import concourse.bass as bass
import concourse.mybir as mybir
from concourse.bass import IndirectOffsetOnAxis
from concourse.bass_utils import run_bass_kernel_spmd

F32 = mybir.dt.float32
BF16 = mybir.dt.bfloat16
I32 = mybir.dt.int32
AF = mybir.ActivationFunctionType
ALU = mybir.AluOpType
AX = mybir.AxisListType

T = 2048
C = 1024
QL = 96
KVL = 32
DH = 4096
SCALE = 1.0 / math.sqrt(96.0)
MAGIC = 1.5 * (2.0 ** 23)
TWO_PI = 2.0 * math.pi
CW1 = float(np.float32(6.28125))
CW2 = float(np.float32(TWO_PI - 6.28125))
CW3 = float(np.float64(TWO_PI) - np.float64(CW1) - np.float64(CW2))
EPS_RMS = 1e-6
EPS_LN = 1e-5

ENGINES = ("sync", "tensor", "vector", "scalar", "gpsimd")
_CACHE = {}


class _Done(Exception):
    def __init__(self, nc):
        self.nc = nc


def build_program():
    try:
        return _build_impl()
    except _Done as d:
        return d.nc


def _build_impl():
    import os
    _stop = os.environ.get("KSTOP", "")
    nc = bass.Bass("TRN2", target_bir_lowering=False)

    def din(name, shape, dt=F32):
        return nc.dram_tensor(name, list(shape), dt, kind="ExternalInput").ap()

    x_tok = din("x_tok", [T, C])
    xT_d = din("xT", [C, T])
    w_sa = din("w_sa", [C, 128])
    w_sb = din("w_sb", [C, 68])
    wdq_d = din("wdq", [QL, 1536])
    wdk_d = din("wdk", [KVL, 512])
    wdv_d = din("wdv", [KVL, 512])
    wwink_d = din("wwink", [C, 1536])
    wwinv_d = din("wwinv", [C, 512])
    wselk_d = din("wselk", [C, 1536])
    wselv_d = din("wselv", [C, 512])
    wproj_d = din("wproj", [512, C])
    w1T_d = din("w1T", [C, DH])
    w2T_d = din("w2T", [DH, C])
    wimp_d = din("wimp", [C, 1])
    gq_d = din("gq", [QL, 1])
    gkv_d = din("gkv", [KVL, 1])
    cw_d = din("cwcol", [128, 96])
    cb_d = din("cbcol", [128, 32])
    c_trow = din("c_trow", [1, T])
    c_drow = din("c_drow", [1, 128])
    c_ident = din("c_ident", [128, 128])
    c_iota256 = din("c_iota256", [128, 256])
    c_tcol = din("c_tcol", [128, 16])
    c_masks = din("c_masks", [128, 2048], BF16)
    c_ones = din("c_ones", [1, 128])

    out_dram = nc.dram_tensor("outT", [C, T], F32, kind="ExternalOutput").ap()
    scr_x1 = nc.dram_tensor("scr_x1", [C, T], F32, kind="Internal").ap()
    scr_z2 = nc.dram_tensor("scr_z2", [C, T], F32, kind="Internal").ap()
    scr_kw = nc.dram_tensor("scr_kw", [12, 128, T], BF16, kind="Internal").ap()
    scr_w2 = nc.dram_tensor("scr_w2", [DH, C], BF16, kind="Internal").ap()

    cms = []

    def salloc(name, shape, dt):
        cm = nc.sbuf_tensor(name, list(shape), dt)
        t = cm.__enter__()
        cms.append(cm)
        return t

    BQ = salloc("BQ", [128, 24576], BF16)
    BK = salloc("BK", [128, 12288], BF16)
    BV = salloc("BV", [128, 16384], BF16)
    BS = salloc("BS", [128, 8192], BF16)
    BA = salloc("BA", [128, 8192], BF16)
    FW = salloc("FW", [128, 6144], F32)
    FR = salloc("FR", [128, 4608], F32)
    FK = salloc("FK", [128, 1024], F32)
    FC = salloc("FC", [128, 64], F32)
    IIX = salloc("IIX", [128, 260], I32)

    xTb = [BV[:, 2048 * k:2048 * (k + 1)] for k in range(8)]
    CA = BS[:, 0:2048]
    SA = BS[:, 2048:4096]
    masks = BS[:, 0:2048]
    nqT = BS[:, 4096:6144]
    krT = BS[:, 6144:8192]
    ckvT = BA[:, 0:2048]
    vslots = [BV[:, 528 * j:528 * (j + 1)] for j in range(16)]
    ksall = [BV[:, 8448 + 256 * j:8448 + 256 * (j + 1)] for j in range(12)]
    vsl = [BV[:, 11520 + 528 * j:11520 + 528 * (j + 1)] for j in range(2)]
    PT = [BV[:, 12576 + 512 * j:12576 + 512 * (j + 1)] for j in range(2)]
    selT = [BV[:, 13600 + 256 * k:13600 + 256 * (k + 1)] for k in range(8)]
    Qg = [[BQ[:, 2048 * (3 * g + p):2048 * (3 * g + p + 1)] for p in range(3)]
          for g in range(4)]
    wsA_b = [BK[:, 128 * k:128 * (k + 1)] for k in range(8)]
    wsB_b = [BK[:, 1024 + 128 * k:1024 + 128 * (k + 1)] for k in range(8)]
    wdq_b = BK[:, 2048:3584]
    wink_b = [BK[:, 1536 * k:1536 * (k + 1)] for k in range(8)]
    winv_b = [BK[:, 512 * k:512 * (k + 1)] for k in range(8)]
    wselk_b = [BK[:, 384 * k:384 * (k + 1)] for k in range(8)]
    wselv_b = [BK[:, 3072 + 512 * k:3072 + 512 * (k + 1)] for k in range(8)]
    kn_g = BK[:, 0:2048]
    kr2tmp = BA[:, 4624:6672]
    krx1 = BK[:, 2048:4096]
    krx2 = BK[:, 4096:6144]
    wdk_b = BK[:, 8192:8704]
    wdv_b = BK[:, 8704:9216]
    kwbuf = [[BK[:, 6144 * h + 2048 * p:6144 * h + 2048 * (p + 1)] for p in range(3)]
             for h in range(2)]
    wproj_b = [BK[:, 8192 + 1024 * g:8192 + 1024 * (g + 1)] for g in range(4)]
    agg = [BA[:, 2048 * g:2048 * (g + 1)] for g in range(4)]
    w1q = [BQ[:, 4096 * k:4096 * (k + 1)] for k in range(6)] + \
          [BK[:, 4096 * k:4096 * (k + 1)] for k in range(2)]
    hsl = [BV[:, 512 * m:512 * (m + 1)] for m in range(32)]
    xq = [BA[:, 2048 * k:2048 * (k + 1)] for k in range(4)] + \
         [BS[:, 2048 * k:2048 * (k + 1)] for k in range(4)]
    w2bf = [BK[:, 8192 + 1024 * j:8192 + 1024 * (j + 1)] for j in range(2)]

    fwv = [FW[:, 512 * i:512 * (i + 1)] for i in range(12)]
    fw2k = [FW[:, 2048 * i:2048 * (i + 1)] for i in range(3)]
    h1f = FW[:, 5120 + 0:5120 + 514]
    R = FR[:, 0:2048]
    F0 = FR[0:1, :]
    F32r = FR[32:33, :]
    ROW_TROW = F0[:, 0:2048]
    ROW_F = F0[:, 2048:2176]
    ROW_D = F0[:, 2176:2304]
    ROW_A = F0[:, 0:2048]
    ROW_B = F0[:, 2048:4096]
    ROW_MEAN = F0[:, 256:768]
    ROW_INV = F0[:, 768:1280]
    ROW_BB = F0[:, 1280:1792]
    ROW_SC = F0[:, 1792:2304]
    ROW_RX = F0[:, 2304:4352]
    ROW_G = F0[:, 4352:4355]
    ROW_S1 = F0[:, 4356:4360]
    ROW_CM = F0[:, 4360:4368]
    sums_mc = FR[:, 4368:4384]
    sums_ic = FR[:, 4384:4400]
    k_ident = FK[:, 0:128]
    k_iota = FK[:, 128:384]
    k_tcol = FK[:, 384:400]
    k_ones = FK[:, 400:528]
    k_cw = FK[:, 528:624]
    k_cb = FK[:, 624:656]
    k_onescol = FK[:, 656:657]
    k_magic = FK[:, 657:658]
    k_negmagic = FK[:, 658:659]

    sems = {}
    for s in ("dma", "gdma", "pe", "dve", "act", "gp"):
        cm = nc.semaphore("sem_" + s)
        sems[s] = cm.__enter__()
        cms.append(cm)

    def palloc(name, shape):
        cm = nc.psum_tensor(name, list(shape), F32)
        t = cm.__enter__()
        cms.append(cm)
        return t

    pBig = palloc("pBig", [128, 2048])
    pB = [pBig[:, 512 * i:512 * (i + 1)] for i in range(4)]
    pS = [palloc("pS0", [128, 512])[:], palloc("pS1", [128, 512])[:]]
    pA = [palloc("pA0", [128, 512])[:], palloc("pA1", [128, 512])[:]]

    q = {e: [] for e in ENGINES}
    cnt = {k: 0 for k in sems}

    def push(eng, waits, build, inc=None, inck=1):
        waits = tuple((s, v) for (s, v) in waits if v > 0)

        def fn(e, waits=waits, build=build, inc=inc, inck=inck):
            for s, v in waits:
                e.wait_ge(sems[s], v)
            ins = build(e)
            if inc is not None and ins is not None:
                ins.then_inc(sems[inc], inck)
        q[eng].append(fn)
        if inc is not None:
            cnt[inc] += inck
        return cnt[inc] if inc else 0

    def at(*ss):
        return [(s, cnt[s]) for s in ss]

    def dma(dst, src, waits=()):
        return push("sync", waits, lambda e: e.dma_start(out=dst, in_=src), "dma", 16)

    def gdma(dst, src, waits=()):
        return push("gpsimd", waits, lambda e: e.dma_start(out=dst, in_=src), "gdma", 16)

    def mm(out, lhsT, rhs, start, stop, tp=None, waits=()):
        return push("tensor", waits,
                    lambda e: nc.tensor.matmul(out, lhsT, rhs, start=start, stop=stop,
                                               tile_position=tp), "pe", 1)

    def tr(out, in_, waits=()):
        return push("tensor", waits, lambda e: nc.tensor.transpose(out, in_, k_ident),
                    "pe", 1)

    def dve(build, waits=()):
        return push("vector", tuple(waits) + (("dve", cnt["dve"]),), build, "dve", 1)

    def act(build, waits=()):
        return push("scalar", waits, build, "act", 1)

    def gp(build, waits=()):
        return push("gpsimd", waits, build, "gp", 1)

    def assemble():
        with nc.Block() as block:
            @block.sync
            def _(eng):
                for fn in q["sync"]:
                    fn(eng)

            @block.tensor
            def _(eng):
                for fn in q["tensor"]:
                    fn(eng)

            @block.vector
            def _(eng):
                for fn in q["vector"]:
                    fn(eng)

            @block.scalar
            def _(eng):
                for fn in q["scalar"]:
                    fn(eng)

            @block.gpsimd
            def _(eng):
                for fn in q["gpsimd"]:
                    fn(eng)
        for cm in reversed(cms):
            try:
                cm.__exit__(None, None, None)
            except Exception:
                pass

    def ckpt(name, aps):
        if _stop != name:
            return
        for nm, ap in aps:
            t = nc.dram_tensor("dbg_" + nm, list(ap.shape), ap.dtype,
                               kind="ExternalOutput").ap()
            dma(t, ap, waits=at("dve", "act", "pe", "gdma", "gp"))
        push("vector", [("dma", cnt["dma"])],
             lambda e: e.memset(FC[:, 63:64], 0.0), "dve", 1)
        assemble()
        raise _Done(nc)

    def bcast_row(pdst_cols, row_ap, waits=()):
        first = True
        for pdst, w0 in pdst_cols:
            W = pdst.shape[-1]
            mm(pdst, k_ones[0:1, :], row_ap[:, w0:w0 + W], True, True,
               waits=waits if first else ())
            first = False
        return cnt["pe"]

    stage_state = {"last": [0, 0]}

    def quant_round(out_ap, src_ap, scale_psum, t0, t1, waits=()):
        dve(lambda e: e.tensor_tensor(out=t0, in0=src_ap, in1=scale_psum, op=ALU.mult),
            waits=waits)
        act(lambda e: e.activation(out=t1, in_=t0, func=AF.Identity, bias=k_magic,
                                   scale=1.0), waits=at("dve"))
        dve(lambda e: e.tensor_copy(out=t0.bitcast(I32), in_=t1.bitcast(I32)),
            waits=at("act"))
        act(lambda e: e.activation(out=out_ap, in_=t0, func=AF.Identity, bias=k_negmagic,
                                   scale=1.0), waits=at("dve"))
        return cnt["act"]

    def load_bf16(dst_bf, src_ap, cols, si, rows=128):
        dk = dma(fw2k[si][0:rows, 0:cols], src_ap, waits=at("dve"))
        dve(lambda e, si=si, cols=cols, rows=rows, dst_bf=dst_bf: e.tensor_copy(
            out=dst_bf, in_=fw2k[si][0:rows, 0:cols]), waits=[("dma", dk)])
        stage_state["last"][si] = cnt["dve"]

    dma(k_ident, c_ident)
    dma(k_iota, c_iota256)
    dma(k_tcol, c_tcol)
    dve(lambda e: e.memset(k_ones, 1.0))
    dma(k_cw, cw_d)
    dma(k_cb, cb_d)
    dma(ROW_TROW, c_trow)
    dma(ROW_D, c_drow)
    dma(FC[0:96, 0:1], gq_d)
    dma(FC[0:32, 1:2], gkv_d)
    c_dma = cnt["dma"]
    dve(lambda e: e.memset(k_onescol, 1.0))
    dve(lambda e: e.memset(k_magic, MAGIC))
    dve(lambda e: e.memset(k_negmagic, -MAGIC))

    cast_done = [0] * 8
    for k in range(8):
        w = [("dve", cast_done[k - 2])] if k >= 2 else []
        dk = dma(fw2k[k % 2][:, :], xT_d[128 * k:128 * (k + 1), :], waits=w)
        dve(lambda e, k=k: e.tensor_copy(out=xTb[k], in_=fw2k[k % 2]), waits=[("dma", dk)])
        cast_done[k] = cnt["dve"]

    act(lambda e: e.activation(out=ROW_F, in_=ROW_D, func=AF.Exp,
                               scale=-math.log(10000.0) / 32.0), waits=[("dma", c_dma)])
    for n in range(4):
        mm(pB[n], ROW_F, ROW_TROW[:, 512 * n:512 * (n + 1)], True, True,
           waits=at("act") + [("dve", cast_done[7])] if n == 0 else ())
    pe_ang = cnt["pe"]
    dve(lambda e: e.tensor_scalar_mul(out=fw2k[2], in0=pBig[:, :], scalar1=1.0 / TWO_PI),
        waits=[("pe", pe_ang)])
    dve(lambda e: e.tensor_scalar_add(out=fw2k[2], in0=fw2k[2], scalar1=MAGIC))
    dve(lambda e: e.tensor_copy(out=fw2k[0].bitcast(I32), in_=fw2k[2].bitcast(I32)))
    dve(lambda e: e.tensor_scalar_add(out=fw2k[0], in0=fw2k[0], scalar1=-MAGIC))
    dve(lambda e: e.tensor_scalar_mul(out=fw2k[1], in0=fw2k[0], scalar1=-CW1))
    dve(lambda e: e.tensor_add(out=fw2k[1], in0=fw2k[1], in1=pBig[:, :]))
    dve(lambda e: e.tensor_scalar_mul(out=fw2k[2], in0=fw2k[0], scalar1=-CW2))
    dve(lambda e: e.tensor_add(out=fw2k[1], in0=fw2k[1], in1=fw2k[2]))
    dve(lambda e: e.tensor_scalar_mul(out=fw2k[2], in0=fw2k[0], scalar1=-CW3))
    dve(lambda e: e.tensor_add(out=fw2k[1], in0=fw2k[1], in1=fw2k[2]))
    ang = cnt["dve"]
    act(lambda e: e.activation(out=SA, in_=fw2k[1], func=AF.Sin, scale=1.0),
        waits=[("dve", ang)])
    dve(lambda e: e.tensor_scalar_add(out=fw2k[2], in0=fw2k[1], scalar1=math.pi / 2.0),
        waits=at("act"))
    dve(lambda e: e.tensor_single_scalar(out=fw2k[0], in_=fw2k[2], scalar=math.pi,
                                         op=ALU.is_gt))
    dve(lambda e: e.tensor_scalar_mul(out=fw2k[0], in0=fw2k[0], scalar1=-TWO_PI))
    dve(lambda e: e.tensor_add(out=fw2k[2], in0=fw2k[2], in1=fw2k[0]))
    act(lambda e: e.activation(out=CA, in_=fw2k[2], func=AF.Sin, scale=1.0),
        waits=at("dve"))
    tables = cnt["act"]
    ckpt("tables", [("CA", CA), ("SA", SA)])

    def rope_cols(out1, out2, x1, x2, ca, sa, w0, W, waits=()):
        P = x1.shape[0]
        a, b = fwv[8][0:P, 0:W], fwv[9][0:P, 0:W]
        s = slice(w0, w0 + W)
        dve(lambda e: e.tensor_tensor(out=a, in0=x1, in1=ca[0:P, s], op=ALU.mult),
            waits=waits)
        dve(lambda e: e.tensor_tensor(out=b, in0=x2, in1=sa[0:P, s], op=ALU.mult))
        dve(lambda e: e.tensor_tensor(out=out1[:, s] if out1.shape[0] == P
                                      else out1[0:P, s],
                                      in0=a, in1=b, op=ALU.subtract))
        dve(lambda e: e.tensor_tensor(out=a, in0=x1, in1=sa[0:P, s], op=ALU.mult))
        dve(lambda e: e.tensor_tensor(out=b, in0=x2, in1=ca[0:P, s], op=ALU.mult))
        dve(lambda e: e.tensor_tensor(out=out2[:, s] if out2.shape[0] == P
                                      else out2[0:P, s],
                                      in0=a, in1=b, op=ALU.add))
        return cnt["dve"]

    for k in range(8):
        load_bf16(wsA_b[k], w_sa[128 * k:128 * (k + 1), :], 128, k % 2)
    wsa = cnt["dve"]
    for tq in range(4):
        for k in range(8):
            mm(pB[tq], wsA_b[k], xTb[k][:, 512 * tq:512 * (tq + 1)], k == 0, k == 7,
               waits=[("dve", wsa)] if (tq == 0 and k == 0) else ())
    peA = cnt["pe"]
    dve(lambda e: e.tensor_copy(out=fw2k[0][0:96, :], in_=pBig[0:96, :]),
        waits=[("pe", peA)])
    dve(lambda e: e.tensor_copy(out=fw2k[1][0:32, :], in_=pBig[96:128, :]))
    rawAB = cnt["dve"]

    def rms_norm(out_bf, src_f32, P, g_col, waits=()):
        act(lambda e: e.activation(out=fw2k[2][0:P, :], in_=src_f32, func=AF.Square,
                                   scale=1.0), waits=waits)
        sq = cnt["act"]
        for tq in range(4):
            m_ = mm(pA[0][0:1, :], k_onescol[0:P, :], fw2k[2][0:P, 512 * tq:512 * (tq + 1)],
                    True, True, waits=[("act", sq), ("dve", cnt["dve"])])
            dve(lambda e, tq=tq: e.tensor_copy(out=ROW_A[:, 512 * tq:512 * (tq + 1)],
                                               in_=pA[0][0:1, :]), waits=[("pe", m_)])
        dve(lambda e, P=P: e.tensor_scalar_mul(out=ROW_A, in0=ROW_A, scalar1=1.0 / P))
        dve(lambda e: e.tensor_scalar_add(out=ROW_A, in0=ROW_A, scalar1=EPS_RMS))
        act(lambda e: e.activation(out=ROW_B[:, 0:2048], in_=ROW_A, func=AF.Sqrt,
                                   scale=1.0), waits=at("dve"))
        dve(lambda e: e.reciprocal(out=ROW_B[:, 0:2048], in_=ROW_B[:, 0:2048]),
            waits=at("act"))
        b_ = bcast_row([(pB[i], 512 * i) for i in range(4)], ROW_B[:, 0:2048],
                       waits=at("dve"))
        dve(lambda e, P=P: e.tensor_tensor(out=fw2k[2][0:P, :], in0=src_f32,
                                           in1=pBig[0:P, :], op=ALU.mult),
            waits=[("pe", b_)])
        dve(lambda e, P=P: e.tensor_single_scalar(out=out_bf, in_=fw2k[2][0:P, :],
                                                  scalar=g_col, op=ALU.mult))
        return cnt["dve"]

    nq_done = rms_norm(nqT[0:96, :], fw2k[0][0:96, :], 96, FC[0:96, 0:1],
                       waits=[("dve", rawAB)])
    ckv_done = rms_norm(ckvT[0:32, :], fw2k[1][0:32, :], 32, FC[0:32, 1:2],
                        waits=[("dve", nq_done)])
    ckpt("p1a", [("nqT", nqT), ("ckvT", ckvT)])

    load_bf16(wdq_b[0:96, 0:512], wdq_d[0:96, 0:512], 512, 0, rows=96)
    load_bf16(wdq_b[0:96, 512:1024], wdq_d[0:96, 512:1024], 512, 1, rows=96)
    load_bf16(wdq_b[0:96, 1024:1536], wdq_d[0:96, 1024:1536], 512, 0, rows=96)
    wdq_done = cnt["dve"]
    for g in range(4):
        for p in range(3):
            for tq in range(4):
                mm(pB[tq], wdq_b[0:96, 384 * g + 128 * p:384 * g + 128 * (p + 1)],
                   nqT[0:96, 512 * tq:512 * (tq + 1)], True, True,
                   waits=([("dve", max(wdq_done, ckv_done)), ("act", tables)]
                          if (g == 0 and p == 0 and tq == 0) else
                          (at("dve") if tq == 0 else ())))
            pq = cnt["pe"]
            if p == 0:
                dve(lambda e, g=g: e.tensor_copy(out=Qg[g][0], in_=pBig[:, :]),
                    waits=[("pe", pq)])
            else:
                dst = fw2k[0] if p == 1 else fw2k[1]
                dve(lambda e, dst=dst: e.tensor_copy(out=dst, in_=pBig[:, :]),
                    waits=[("pe", pq)])
        for w0 in range(0, T, 512):
            rope_cols(Qg[g][1], Qg[g][2], fw2k[0][:, w0:w0 + 512],
                      fw2k[1][:, w0:w0 + 512], CA, SA, w0, 512)
    q_done = cnt["dve"]
    ckpt("q", [("q00", Qg[0][0]), ("q01", Qg[0][1]), ("q02", Qg[0][2]),
               ("q30", Qg[3][0]), ("q31", Qg[3][1]), ("q32", Qg[3][2])])

    for k in range(8):
        load_bf16(wsB_b[k][:, 0:68], w_sb[128 * k:128 * (k + 1), :], 68, k % 2)
    wsb = cnt["dve"]
    for tq in range(4):
        for k in range(8):
            mm(pB[tq][0:64, :], wsB_b[k][:, 0:64], xTb[k][:, 512 * tq:512 * (tq + 1)],
               k == 0, k == 7, waits=[("dve", wsb)] if (tq == 0 and k == 0) else ())
    peB = cnt["pe"]
    act(lambda e: e.activation(out=fw2k[0][0:32, :], in_=pBig[0:32, :], func=AF.Copy,
                               scale=1.0 / 16.0), waits=[("pe", peB), ("dve", q_done)])
    act(lambda e: e.activation(out=fw2k[1][0:32, :], in_=pBig[32:64, :], func=AF.Copy,
                               scale=1.0 / 16.0))
    kr_ev = cnt["act"]
    for w0 in range(0, T, 512):
        rope_cols(krT[0:32, :], kr2tmp[0:32, :], fw2k[0][0:32, w0:w0 + 512],
                  fw2k[1][0:32, w0:w0 + 512], CA, SA, w0, 512,
                  waits=[("act", kr_ev)] if w0 == 0 else ())
    kr_done = cnt["dve"]
    dwi = dma(FC[:, 32:33], wimp_d[0:128, :], waits=at("dve"))
    for k in range(1, 8):
        dwi = dma(FC[:, 32 + k:33 + k], wimp_d[128 * k:128 * (k + 1), :])
    last_mm = [0]
    for k in range(8):
        dx = dma(fw2k[k % 2][:, :], xT_d[128 * k:128 * (k + 1), :],
                 waits=[("pe", last_mm[0]), ("dve", kr_done), ("act", kr_ev),
                        ("dma", dwi)])
        for tq in range(4):
            mm(pB[tq][0:1, :], FC[:, 32 + k:33 + k],
               fw2k[k % 2][:, 512 * tq:512 * (tq + 1)], k == 0, k == 7,
               waits=[("dma", dx)] if tq == 0 else ())
        if k >= 1:
            last_mm[0] = cnt["pe"] - 4
    peI = cnt["pe"]
    dve(lambda e: e.tensor_copy(out=ROW_A, in_=pBig[0:1, :]), waits=[("pe", peI)])
    scores_done = cnt["dve"]
    for tq in range(4):
        for k in range(8):
            mm(pS[1][0:3, :], wsB_b[k][:, 64:67], xTb[k][:, 512 * tq:512 * (tq + 1)],
               k == 0, k == 7,
               waits=[("dve", scores_done)] if (tq == 0 and k == 0) else
               (at("dve") if k == 0 else ()))
        peG = cnt["pe"]
        dve(lambda e, tq=tq: e.tensor_reduce(out=FC[0:3, 3 + tq:4 + tq],
                                             in_=pS[1][0:3, :], axis=AX.X, op=ALU.add),
            waits=[("pe", peG)])
    dve(lambda e: e.tensor_add(out=FC[0:3, 3:4], in0=FC[0:3, 3:4], in1=FC[0:3, 4:5]))
    dve(lambda e: e.tensor_add(out=FC[0:3, 5:6], in0=FC[0:3, 5:6], in1=FC[0:3, 6:7]))
    dve(lambda e: e.tensor_add(out=FC[0:3, 2:3], in0=FC[0:3, 3:4], in1=FC[0:3, 5:6]))
    dve(lambda e: e.tensor_scalar_mul(out=FC[0:3, 2:3], in0=FC[0:3, 2:3],
                                      scalar1=1.0 / T))
    gd = gdma(ROW_G, FC[0:3, 2:3], waits=at("dve"))
    dve(lambda e: e.tensor_reduce(out=ROW_S1[:, 0:1], in_=ROW_G, axis=AX.X, op=ALU.max),
        waits=[("gdma", gd)])
    dve(lambda e: e.tensor_single_scalar(out=ROW_G, in_=ROW_G,
                                         scalar=ROW_S1[:, 0:1], op=ALU.subtract),
        waits=at("dve"))
    act(lambda e: e.activation(out=ROW_G, in_=ROW_G, func=AF.Exp, scale=1.0),
        waits=at("dve"))
    dve(lambda e: e.tensor_reduce(out=ROW_S1[:, 0:1], in_=ROW_G, axis=AX.X, op=ALU.add),
        waits=at("act"))
    dve(lambda e: e.reciprocal(out=ROW_S1[:, 0:1], in_=ROW_S1[:, 0:1]))
    dve(lambda e: e.tensor_single_scalar(out=ROW_G, in_=ROW_G,
                                         scalar=ROW_S1[:, 0:1], op=ALU.mult),
        waits=at("dve"))
    gate_done = cnt["dve"]
    gmove = gdma(F32r[:, 1024:1027], ROW_G, waits=[("dve", gate_done)])
    ckpt("p1c", [("krT", krT), ("gate", ROW_G), ("scores", ROW_A),
                 ("gfc", FC[0:3, 0:8]), ("grow", F0[:, 4340:4368]), ("fc40", FC[:, 40:48])])

    for k in range(8):
        load_bf16(wink_b[k][:, 0:768], wwink_d[128 * k:128 * (k + 1), 0:768], 768, k % 2)
        load_bf16(wink_b[k][:, 768:1536], wwink_d[128 * k:128 * (k + 1), 768:1536],
                  768, k % 2)
    wink_done = cnt["dve"]
    ckpt("kwl", [("w0", wink_b[0]), ("w7", wink_b[7])])
    kwst = [BA[:, 2048 + 512 * j:2048 + 512 * (j + 1)] for j in range(4)]
    kwst_free = [0, 0, 0, 0]
    for g in range(4):
        for tq in range(4):
            for k in range(8):
                mm(pB[tq], wink_b[k][:, 384 * g:384 * g + 128],
                   xTb[k][:, 512 * tq:512 * (tq + 1)], k == 0, k == 7,
                   waits=([("dve", max(wink_done, scores_done)), ("gdma", gmove)]
                          if (g == 0 and tq == 0 and k == 0) else
                          (at("dve") if (tq == 0 and k == 0) else ())))
        pk = cnt["pe"]
        for tq in range(4):
            dve(lambda e, tq=tq: e.tensor_copy(out=kwst[tq], in_=pB[tq]),
                waits=[("pe", pk), ("gdma", kwst_free[tq])])
            kwst_free[tq] = gdma(scr_kw[3 * g + 0][:, 512 * tq:512 * (tq + 1)], kwst[tq],
                                 waits=at("dve"))
        if g == 0:
            ckpt("kwn", [("kwn0", kwst[0]), ("kwn3", kwst[3])])
        for p in (1, 2):
            for tq in range(4):
                for k in range(8):
                    mm(pB[tq], wink_b[k][:, 384 * g + 128 * p:384 * g + 128 * (p + 1)],
                       xTb[k][:, 512 * tq:512 * (tq + 1)], k == 0, k == 7,
                       waits=at("dve") if (tq == 0 and k == 0) else ())
            pk = cnt["pe"]
            dst = fw2k[0] if p == 1 else fw2k[1]
            dve(lambda e, dst=dst: e.tensor_copy(out=dst, in_=pBig[:, :]),
                waits=[("pe", pk)])
        for tq in range(4):
            w0 = 512 * tq
            for which in range(2):
                dvw = [("gdma", kwst_free[2 * (tq % 2) + which])]
                a, b = fwv[8], fwv[9]
                st = kwst[2 * (tq % 2) + which]
                if which == 0:
                    dve(lambda e, w0=w0: e.tensor_tensor(out=fwv[8][:, 0:512],
                        in0=fw2k[0][:, w0:w0 + 512], in1=CA[:, w0:w0 + 512], op=ALU.mult))
                    dve(lambda e, w0=w0: e.tensor_tensor(out=fwv[9][:, 0:512],
                        in0=fw2k[1][:, w0:w0 + 512], in1=SA[:, w0:w0 + 512], op=ALU.mult))
                    dve(lambda e, st=st: e.tensor_tensor(out=st, in0=fwv[8][:, 0:512],
                        in1=fwv[9][:, 0:512], op=ALU.subtract), waits=dvw)
                else:
                    dve(lambda e, w0=w0: e.tensor_tensor(out=fwv[8][:, 0:512],
                        in0=fw2k[0][:, w0:w0 + 512], in1=SA[:, w0:w0 + 512], op=ALU.mult))
                    dve(lambda e, w0=w0: e.tensor_tensor(out=fwv[9][:, 0:512],
                        in0=fw2k[1][:, w0:w0 + 512], in1=CA[:, w0:w0 + 512], op=ALU.mult))
                    dve(lambda e, st=st: e.tensor_tensor(out=st, in0=fwv[8][:, 0:512],
                        in1=fwv[9][:, 0:512], op=ALU.add), waits=dvw)
                kwst_free[2 * (tq % 2) + which] = gdma(
                    scr_kw[3 * g + 1 + which][:, w0:w0 + 512], st, waits=at("dve"))
    kw_done = cnt["gdma"]
    ckpt("kw", [("kw0", kwst[0]), ("kw1", kwst[1])])

    scr_vw = nc.dram_tensor("scr_vw", [16, 128, 528], BF16, kind="Internal").ap()
    for k in range(8):
        load_bf16(winv_b[k], wwinv_d[128 * k:128 * (k + 1), :], 512, k % 2)
    wvw = cnt["dve"]
    vwst = BA[:, 4096:4624]
    vwst_free = 0
    for j in range(16):
        for k in range(8):
            mm(pS[j % 2], xTb[k][:, 128 * j:128 * (j + 1)], winv_b[k], k == 0, k == 7,
               waits=[("dve", wvw)] if (j == 0 and k == 0) else
                     (at("dve") if k == 0 else ()))
        pv = cnt["pe"]
        dve(lambda e: e.memset(vwst[:, :], 0.0), waits=[("gdma", vwst_free)])
        dve(lambda e: e.memset(
            vwst.rearrange("p (h c) -> p h c", c=33)[:, :, 32:33], 1.0))
        dve(lambda e, j=j: e.tensor_copy(
            out=vwst.rearrange("p (h c) -> p h c", c=33)[:, :, 0:32],
            in_=pS[j % 2].rearrange("p (h c) -> p h c", c=32)), waits=[("pe", pv)])
        vwst_free = gdma(scr_vw[j], vwst[:, :], waits=at("dve"))
    vw_done = cnt["gdma"]
    ckpt("vw", [("vwst", vwst)])

    bs_ = bcast_row([(pB[i], 512 * i) for i in range(4)], ROW_A,
                    waits=[("dve", scores_done), ("gdma", vw_done)])
    dve(lambda e: e.tensor_copy(out=fw2k[2], in_=pBig[:, :]), waits=[("pe", bs_)])
    dve(lambda e: e.tensor_copy(out=fw2k[0], in_=fw2k[2]))
    for i in range(32):
        dve(lambda e: e.max(out=FC[:, 24:32], in_=fw2k[0]))
        dve(lambda e: e.tensor_single_scalar(out=fw2k[1], in_=fw2k[0],
                                             scalar=FC[:, 31:32], op=ALU.is_ge),
            waits=at("dve"))
        dve(lambda e: e.tensor_scalar_mul(out=fw2k[1], in0=fw2k[1], scalar1=-1e30))
        dve(lambda e: e.tensor_add(out=fw2k[0], in0=fw2k[0], in1=fw2k[1]))
    dve(lambda e: e.tensor_tensor(out=fw2k[1], in0=fw2k[2], in1=fw2k[0],
                                  op=ALU.is_gt))
    dve(lambda e: e.memset(fw2k[0], 0.0))
    dve(lambda e: e.tensor_tensor_scan(out=fw2k[2], data0=fw2k[1], data1=fw2k[0],
                                       initial=0.0, op0=ALU.add, op1=ALU.add))
    tk_rows = cnt["dve"]
    dma(sums_mc, fw2k[1][0:1, :].rearrange("one (p c) -> one p c", p=128),
        waits=[("dve", tk_rows)])
    oh_dma = dma(sums_ic, fw2k[2][0:1, :].rearrange("one (p c) -> one p c", p=128))
    for c in range(16):
        w = [("dma", oh_dma)] if c == 0 else [("pe", cnt["pe"])]
        dve(lambda e, c=c: e.tensor_single_scalar(out=fwv[c % 2][:, 0:256], in_=k_iota,
                                                  scalar=sums_ic[:, c:c + 1],
                                                  op=ALU.is_equal), waits=w)
        dve(lambda e, c=c: e.tensor_single_scalar(out=fwv[c % 2][:, 0:256],
                                                  in_=fwv[c % 2][:, 0:256],
                                                  scalar=sums_mc[:, c:c + 1],
                                                  op=ALU.mult))
        dve(lambda e, c=c: e.tensor_scalar_add(out=fwv[2 + c % 2][:, 0:1],
                                               in0=k_tcol[:, c:c + 1], scalar1=float(c)))
        mm(pS[0][0:1, 0:256], fwv[2 + c % 2][:, 0:1], fwv[c % 2][:, 0:256],
           c == 0, c == 15, waits=at("dve"))
    pidx = cnt["pe"]
    dve(lambda e: e.tensor_copy(out=ROW_B[:, 0:256], in_=pS[0][0:1, 0:256]),
        waits=[("pe", pidx)])
    dve(lambda e: e.tensor_copy(out=IIX[0:1, 0:256], in_=ROW_B[:, 0:256]),
        waits=at("dve"))
    idx_done = cnt["dve"]
    ckpt("topk", [("idx", ROW_B[:, 0:256]), ("mask", fw2k[1][0:1, :])])
    dma(IIX[:, 256:257], IIX[0:1, 0:128], waits=[("dve", idx_done)])
    dscat = dma(IIX[:, 257:258], IIX[0:1, 128:256])
    push("gpsimd", [("dma", dscat)],
         lambda e: e.indirect_dma_start(out=fw2k[0][:, 0:1024], out_offset=None,
                                        in_=x_tok,
                                        in_offset=IndirectOffsetOnAxis(IIX[:, 256:257], 0)),
         "gdma", 16)
    gath = push("gpsimd", [],
                lambda e: e.indirect_dma_start(out=fw2k[1][:, 0:1024], out_offset=None,
                                               in_=x_tok,
                                               in_offset=IndirectOffsetOnAxis(IIX[:, 257:258], 0)),
                "gdma", 16)
    for i in range(2):
        for k in range(8):
            t_ = tr(pS[1][:, 0:128], fw2k[i][:, 128 * k:128 * (k + 1)],
                    waits=([("gdma", gath)] if (i == 0 and k == 0) else []) + at("dve"))
            dve(lambda e, i=i, k=k: e.tensor_copy(out=selT[k][:, 128 * i:128 * (i + 1)],
                                                  in_=pS[1][:, 0:128]), waits=[("pe", t_)])
    sel_done = cnt["dve"]
    ckpt("sel", [("sel0", selT[0]), ("sel7", selT[7])])

    for g in range(4):
        for k in range(8):
            load_bf16(wselk_b[k][:, 0:384],
                      wselk_d[128 * k:128 * (k + 1), 384 * g:384 * (g + 1)], 384, k % 2)
        wsel = cnt["dve"]
        for p in range(3):
            for k in range(8):
                mm(pBig[:, 512 * p:512 * p + 256],
                   wselk_b[k][:, 128 * p:128 * (p + 1)], selT[k], k == 0, k == 7,
                   waits=[("dve", wsel)] if (p == 0 and k == 0) else ())
        pks = cnt["pe"]
        dve(lambda e, g=g: e.tensor_copy(out=ksall[3 * g + 0], in_=pBig[:, 0:256]),
            waits=[("pe", pks)])
        dve(lambda e: e.tensor_copy(out=fwv[4][:, 0:256], in_=pBig[:, 512:768]))
        dve(lambda e: e.tensor_copy(out=fwv[5][:, 0:256], in_=pBig[:, 1024:1280]))
        rope_cols(ksall[3 * g + 1], ksall[3 * g + 2], fwv[4][:, 0:256], fwv[5][:, 0:256],
                  CA, SA, 0, 256)
    ks_done = cnt["dve"]
    for k in range(8):
        load_bf16(wselv_b[k], wselv_d[128 * k:128 * (k + 1), :], 512, k % 2)
    wsv = cnt["dve"]
    dve(lambda e: e.memset(vsl[0][:, :], 0.0))
    dve(lambda e: e.memset(vsl[1][:, :], 0.0))
    dve(lambda e: e.memset(vsl[0].rearrange("p (h c) -> p h c", c=33)[:, :, 32:33], 1.0))
    dve(lambda e: e.memset(vsl[1].rearrange("p (h c) -> p h c", c=33)[:, :, 32:33], 1.0))
    vsl_init = cnt["dve"]
    for i in range(2):
        for k in range(8):
            mm(pS[i], selT[k][:, 128 * i:128 * (i + 1)], wselv_b[k], k == 0, k == 7,
               waits=[("dve", max(wsv, vsl_init))] if (i == 0 and k == 0) else ())
        pvs = cnt["pe"]
        dve(lambda e, i=i: e.tensor_copy(
            out=vsl[i].rearrange("p (h c) -> p h c", c=33)[:, :, 0:32],
            in_=pS[i].rearrange("p (h c) -> p h c", c=32)), waits=[("pe", pvs)])
    vs_done = cnt["dve"]
    mask_dma = dma(masks, c_masks, waits=[("dve", max(ks_done, vs_done))])
    ckpt("p2", [("ks0", ksall[0]), ("ks1", ksall[1]), ("ks2", ksall[2]),
                ("vsl0", vsl[0]), ("vsl1", vsl[1])])

    def bound(kparts, kP, kW, g, branch, qparts):
        col = 24 + (branch * 4 + g) % 8
        for ip, qp in enumerate(qparts):
            dve(lambda e, qp=qp, ip=ip: e.tensor_reduce(
                out=fwv[10][:, ip:ip + 1], in_=qp, axis=AX.X, op=ALU.max,
                apply_absolute_value=True))
        dve(lambda e: e.tensor_tensor(out=fwv[10][:, 0:1], in0=fwv[10][:, 0:1],
                                      in1=fwv[10][:, 1:2], op=ALU.max))
        dve(lambda e: e.tensor_tensor(out=fwv[10][:, 0:1], in0=fwv[10][:, 0:1],
                                      in1=fwv[10][:, 2:3], op=ALU.max))
        qm = cnt["dve"]
        t_ = tr(pS[1][0:1, 0:128], fwv[10][:, 0:1], waits=[("dve", qm)] + at("act"))
        dve(lambda e: e.tensor_reduce(out=ROW_S1[:, 1:2], in_=pS[1][0:1, 0:128],
                                      axis=AX.X, op=ALU.max), waits=[("pe", t_)])
        for ip, (kp, nr) in enumerate(kparts):
            dve(lambda e, kp=kp, nr=nr, ip=ip: e.tensor_reduce(
                out=fwv[10][0:nr, 3 + ip:4 + ip], in_=kp, axis=AX.X, op=ALU.max,
                apply_absolute_value=True))
        if True:
            dve(lambda e: e.tensor_tensor(out=fwv[10][:, 3:4], in0=fwv[10][:, 3:4],
                                          in1=fwv[10][:, 4:5], op=ALU.max))
            dve(lambda e: e.tensor_tensor(out=fwv[10][:, 3:4], in0=fwv[10][:, 3:4],
                                          in1=fwv[10][:, 5:6], op=ALU.max))
            km = cnt["dve"]
            t2 = tr(pS[1][0:1, 0:128], fwv[10][:, 3:4], waits=[("dve", km)] + at("act"))
            dve(lambda e: e.tensor_reduce(out=ROW_S1[:, 2:3], in_=pS[1][0:1, 0:128],
                                          axis=AX.X, op=ALU.max), waits=[("pe", t2)])
        else:
            raise AssertionError("unused")
        dve(lambda e: e.tensor_tensor(out=ROW_S1[:, 3:4], in0=ROW_S1[:, 1:2],
                                      in1=ROW_S1[:, 2:3], op=ALU.mult), waits=at("dve"))
        dve(lambda e: e.tensor_scalar_mul(out=ROW_S1[:, 3:4], in0=ROW_S1[:, 3:4],
                                          scalar1=-96.0 * SCALE))
        ub = cnt["dve"]
        m_ = mm(pS[1][:, 0:1], k_ones[0:1, :], ROW_S1[:, 3:4], True, True,
                waits=[("dve", ub)])
        dve(lambda e, col=col: e.tensor_copy(out=FC[:, col:col + 1], in_=pS[1][:, 0:1]),
            waits=[("pe", m_)])
        return FC[:, col:col + 1], cnt["dve"]

    def sweep(branch, g, kparts, kcolw, nkt_of_qc, vget, bexp, bexp_done, causal,
              entry_waits):
        last_exp = [0, 0]
        free_d = [cnt["dve"], cnt["dve"]]
        free_g = [0, 0]
        first = True
        for qc in range(4):
            ktmax = nkt_of_qc(qc)
            for pr in range(2):
                for kt in range(ktmax):
                    for jj in range(2):
                        hl = 2 * pr + jj
                        b = 32 * hl
                        h = 4 * g + hl
                        kc = slice(kcolw * kt, kcolw * kt + 128)
                        qs = slice(512 * qc, 512 * (qc + 1))
                        w = [("act", last_exp[jj])]
                        if first:
                            w += entry_waits + [("dve", bexp_done)]
                            first = False
                        mm(pS[jj], kparts[0][b:b + 32, kc], Qg[g][0][b:b + 32, qs],
                           True, False, tp=(b, 0), waits=w)
                        mm(pS[jj], kparts[1][b:b + 32, kc], Qg[g][1][b:b + 32, qs],
                           False, False, tp=(b, 0))
                        qk = mm(pS[jj], kparts[2][b:b + 32, kc], Qg[g][2][b:b + 32, qs],
                                False, True, tp=(b, 0))
                        push("scalar", [("pe", qk)],
                             lambda e, jj=jj, bexp=bexp: e.activation(
                                 out=PT[jj], in_=pS[jj], func=AF.Exp, bias=bexp,
                                 scale=SCALE), "act", 1)
                        last_exp[jj] = cnt["act"]
                        pdep = ("act", last_exp[jj])
                        if causal and kt >= 4 * qc:
                            i = kt - 4 * qc
                            dve(lambda e, jj=jj, i=i: e.tensor_tensor(
                                out=PT[jj], in0=PT[jj], in1=masks[:, 512 * i:512 * (i + 1)],
                                op=ALU.mult), waits=[pdep])
                            pdep = ("dve", cnt["dve"])
                        avw = [pdep]
                        if kt == 0:
                            avw += [("dve", free_d[jj]), ("gdma", free_g[jj])]
                        mm(pA[jj][0:33, :], vget(kt)[:, 33 * h:33 * h + 33], PT[jj],
                           kt == 0, kt == ktmax - 1, waits=avw)
                for jj in range(2):
                    hl = 2 * pr + jj
                    b = 32 * hl
                    avp = cnt["pe"]
                    frow = F32r[:, 512 * jj:512 * jj + 512]
                    dve(lambda e, frow=frow, jj=jj: e.tensor_copy(
                        out=frow, in_=pA[jj][32:33, :]), waits=[("pe", avp)])
                    sg_ = cnt["gdma"]
                    dve(lambda e, frow=frow: e.reciprocal(out=frow, in_=frow))
                    dve(lambda e, frow=frow, branch=branch: e.tensor_single_scalar(
                        out=frow, in_=frow,
                        scalar=F32r[:, 1024 + branch:1025 + branch], op=ALU.mult),
                        waits=at("dve") + [("gdma", gmove)])
                    fb = cnt["dve"]
                    m_ = mm(pB[jj][b:b + 32, :], k_ones[32:33, 0:32], frow,
                            True, True, tp=(32, b), waits=[("dve", fb)])
                    dve(lambda e, jj=jj: e.tensor_copy(out=fwv[4 + jj][0:32, :],
                                                       in_=pA[jj][0:32, :]))
                    free_d[jj] = cnt["dve"]
                    free_g[jj] = sg_
                    mv = gdma(fwv[6 + jj][b:b + 32, :], fwv[4 + jj][0:32, :],
                              waits=at("dve"))
                    qs = slice(512 * qc, 512 * (qc + 1))
                    if branch == 0:
                        dve(lambda e, g=g, b=b, jj=jj, qs=qs: e.tensor_tensor(
                            out=agg[g][b:b + 32, qs], in0=fwv[6 + jj][b:b + 32, :],
                            in1=pB[jj][b:b + 32, :], op=ALU.mult),
                            waits=[("gdma", mv), ("pe", m_)])
                    else:
                        dve(lambda e, b=b, jj=jj: e.tensor_tensor(
                            out=fwv[8 + jj][b:b + 32, :], in0=fwv[6 + jj][b:b + 32, :],
                            in1=pB[jj][b:b + 32, :], op=ALU.mult),
                            waits=[("gdma", mv), ("pe", m_)])
                        dve(lambda e, g=g, b=b, jj=jj, qs=qs: e.tensor_tensor(
                            out=agg[g][b:b + 32, qs], in0=agg[g][b:b + 32, qs],
                            in1=fwv[8 + jj][b:b + 32, :], op=ALU.add))
        return cnt["gp"]

    a1_pe_gate = cnt["pe"]
    for si, (dst, src) in enumerate(((wdk_b, wdk_d), (wdv_b, wdv_d))):
        dk = dma(fwv[si][:, 0:512][0:32, :], src, waits=[("gdma", kw_done)])
        dve(lambda e, dst=dst, si=si: e.tensor_copy(out=dst[0:32, :],
                                                    in_=fwv[si][0:32, 0:512]),
            waits=[("dma", dk)])
    wdkv = cnt["dve"]
    for hl in range(4):
        gdma(krx1[32 * hl:32 * hl + 32, :], krT[0:32, :], waits=[("dve", kr_done)])
        gdma(krx2[32 * hl:32 * hl + 32, :], kr2tmp[0:32, :])
    krx_done = cnt["gdma"]
    ckv2 = BS[:, 4096:6144]
    dve(lambda e: e.tensor_copy(out=ckv2[0:32, :], in_=ckvT[0:32, :]),
        waits=[("dve", ckv_done), ("pe", cnt["pe"])])
    ckv2_done = cnt["dve"]
    for j in range(16):
        dve(lambda e, j=j: e.memset(vslots[j][:, :], 0.0),
            waits=[("gdma", max(vw_done, gath))] if j == 0 else ())
        dve(lambda e, j=j: e.memset(
            vslots[j].rearrange("p (h c) -> p h c", c=33)[:, :, 32:33], 1.0))
        vz = cnt["dve"]
        pv = mm(pS[j % 2], ckv2[0:32, 128 * j:128 * (j + 1)], wdv_b[0:32, :],
                True, True, waits=[("dve", max(wdkv, vz, ckv2_done))])
        dve(lambda e, j=j: e.tensor_copy(
            out=vslots[j].rearrange("p (h c) -> p h c", c=33)[:, :, 0:32],
            in_=pS[j % 2].rearrange("p (h c) -> p h c", c=32)), waits=[("pe", pv)])
    v1_done = cnt["dve"]

    for g in range(4):
        for tq in range(4):
            mm(pB[tq], wdk_b[0:32, 128 * g:128 * (g + 1)],
               ckv2[0:32, 512 * tq:512 * (tq + 1)], True, True,
               waits=[("dve", max(v1_done, cnt["dve"])), ("act", cnt["act"])]
               if tq == 0 else ())
        pk = cnt["pe"]
        dve(lambda e: e.tensor_copy(out=kn_g[:, :], in_=pBig[:, :]), waits=[("pe", pk)])
        kn_done = cnt["dve"]
        bexp, bdone = bound([(kn_g, 128), (krx1, 128), (krx2, 128)], 128, 2048, g, 0,
                            Qg[g])
        sweep(0, g, [kn_g, krx1, krx2], 128, lambda qc: 4 * qc + 4,
              lambda kt: vslots[kt], bexp, bdone, True,
              [("dve", kn_done), ("gdma", krx_done), ("dma", mask_dma)])

    ckpt("a1", [("agg0", agg[0]), ("agg3", agg[3])])

    for g in range(4):
        kp = [ksall[3 * g + 0], ksall[3 * g + 1], ksall[3 * g + 2]]
        bexp, bdone = bound([(k, 128) for k in kp], 128, 256, g, 1, Qg[g])
        sweep(1, g, kp, 128, lambda qc: 2, lambda kt: vsl[kt], bexp, bdone, False,
              [("dve", vs_done)])

    ckpt("a2", [("agg0", agg[0]), ("agg3", agg[3])])

    for j in range(16):
        gdma(vslots[j][:, :], scr_vw[j], waits=[("gp", cnt["gp"])] if j == 0 else ())
    vw_re = cnt["gdma"]
    for g in range(4):
        kwdma = 0
        for p in range(3):
            kwdma = dma(kwbuf[0][p], scr_kw[3 * g + p],
                        waits=[("act", cnt["act"])] if p == 0 else ())
        if g == 0:
            ckpt("a3pre", [("kwb0", kwbuf[0][0]), ("kwb1", kwbuf[0][1]),
                           ("kwb2", kwbuf[0][2]), ("vs0", vslots[0])])
        kp = kwbuf[0]
        bexp, bdone = bound([(k, 128) for k in kp], 128, 2048, g, 2, Qg[g])
        sweep(2, g, kp, 128, lambda qc: 4 * qc + 4, lambda kt: vslots[kt], bexp, bdone,
              True, [("dma", kwdma), ("gdma", vw_re)])
        if g == 0:
            ckpt("a3g0", [("agg0", agg[0])])
    att_done_gp = cnt["gp"]
    ckpt("a3", [("agg0", agg[0]), ("agg1", agg[1]), ("agg2", agg[2]),
                ("agg3", agg[3])])

    for kx in range(4):
        load_bf16(wproj_b[kx], wproj_d[128 * kx:128 * (kx + 1), :], 1024, kx % 2)
    wpj = cnt["dve"]
    zq = [FW[:, 512 * i:512 * (i + 1)] for i in range(8)]
    scr_dma = [0] * 8
    for qc in range(4):
        qs = slice(512 * qc, 512 * (qc + 1))
        for mo in range(8):
            for kx in range(4):
                mm(pS[mo % 2], wproj_b[kx][:, 128 * mo:128 * (mo + 1)], agg[kx][:, qs],
                   kx == 0, kx == 3,
                   waits=[("dve", wpj), ("gp", att_done_gp)] if (qc == 0 and mo == 0 and kx == 0)
                   else ([("dve", cnt["dve"])] if kx == 0 else ()))
            pp = cnt["pe"]
            dx = dma(fwv[8 + mo % 2][:, :], xT_d[128 * mo:128 * (mo + 1), qs],
                     waits=at("dve"))
            dve(lambda e, mo=mo: e.tensor_add(out=zq[mo], in0=fwv[8 + mo % 2][:, :],
                                              in1=pS[mo % 2]),
                waits=[("pe", pp), ("dma", dx), ("dma", scr_dma[mo])])
        z_done = cnt["dve"]
        for mo in range(8):
            act(lambda e, mo=mo: e.activation(out=fwv[10][:, :], in_=zq[mo],
                                              func=AF.Square, scale=1.0),
                waits=[("dve", z_done)] if mo == 0 else at("pe"))
            sq = cnt["act"]
            mm(pA[0][0:1, :], k_onescol, zq[mo], mo == 0, mo == 7,
               waits=[("act", sq)])
            mm(pA[1][0:1, :], k_onescol, fwv[10][:, :], mo == 0, mo == 7)
        pst = cnt["pe"]
        dve(lambda e: e.tensor_scalar_mul(out=ROW_MEAN, in0=pA[0][0:1, :],
                                          scalar1=1.0 / C), waits=[("pe", pst)])
        dve(lambda e: e.tensor_scalar_mul(out=ROW_INV, in0=pA[1][0:1, :],
                                          scalar1=1.0 / C))
        dve(lambda e: e.tensor_tensor(out=ROW_BB, in0=ROW_MEAN, in1=ROW_MEAN,
                                      op=ALU.mult))
        dve(lambda e: e.tensor_tensor(out=ROW_INV, in0=ROW_INV, in1=ROW_BB,
                                      op=ALU.subtract))
        dve(lambda e: e.tensor_scalar_add(out=ROW_INV, in0=ROW_INV, scalar1=EPS_LN))
        act(lambda e: e.activation(out=ROW_INV, in_=ROW_INV, func=AF.Sqrt, scale=1.0),
            waits=at("dve"))
        dve(lambda e: e.reciprocal(out=ROW_INV, in_=ROW_INV), waits=at("act"))
        dve(lambda e: e.tensor_tensor(out=ROW_BB, in0=ROW_MEAN, in1=ROW_INV,
                                      op=ALU.mult))
        dve(lambda e: e.tensor_scalar_mul(out=ROW_BB, in0=ROW_BB, scalar1=-1.0))
        rows_ab = cnt["dve"]
        mm(pB[2], k_ones[0:1, :], ROW_INV, True, True, waits=[("dve", rows_ab)])
        ab_pe = mm(pB[3], k_ones[0:1, :], ROW_BB, True, True)
        for mo in range(8):
            dve(lambda e, mo=mo: e.tensor_tensor(out=fwv[10][:, :], in0=zq[mo],
                                                 in1=pB[2], op=ALU.mult),
                waits=[("pe", ab_pe)] if mo == 0 else ())
            dve(lambda e, mo=mo: e.tensor_add(out=zq[mo], in0=fwv[10][:, :], in1=pB[3]))
            scr_dma[mo] = dma(scr_x1[128 * mo:128 * (mo + 1), qs], zq[mo],
                              waits=at("dve"))
        x1_done = cnt["dve"]
        dve(lambda e: e.tensor_copy(out=fwv[10][:, :], in_=zq[0]),
            waits=[("dve", x1_done)])
        dve(lambda e: e.tensor_copy(out=fwv[11][:, :], in_=zq[0]))
        for mo in range(1, 8):
            dve(lambda e, mo=mo: e.tensor_tensor(out=fwv[10][:, :], in0=fwv[10][:, :],
                                                 in1=zq[mo], op=ALU.max))
            dve(lambda e, mo=mo: e.tensor_tensor(out=fwv[11][:, :], in0=fwv[11][:, :],
                                                 in1=zq[mo], op=ALU.min))
        dve(lambda e: e.tensor_scalar_mul(out=fwv[11][:, :], in0=fwv[11][:, :],
                                          scalar1=-1.0))
        dve(lambda e: e.tensor_tensor(out=fwv[10][:, :], in0=fwv[10][:, :],
                                      in1=fwv[11][:, :], op=ALU.max))
        am = cnt["dve"]
        for j in range(4):
            t1 = tr(pA[0][:, 0:128], fwv[10][:, 128 * j:128 * (j + 1)],
                    waits=[("dve", am)])
            dve(lambda e, j=j: e.tensor_reduce(out=fwv[11][:, j:j + 1],
                                               in_=pA[0][:, 0:128], axis=AX.X,
                                               op=ALU.max), waits=[("pe", t1)])
            t2 = tr(pA[1][0:1, 0:128], fwv[11][:, j:j + 1], waits=at("dve"))
            dve(lambda e, j=j: e.tensor_copy(out=ROW_SC[:, 128 * j:128 * (j + 1)],
                                             in_=pA[1][0:1, 0:128]), waits=[("pe", t2)])
        dve(lambda e: e.tensor_scalar_max(out=ROW_SC, in0=ROW_SC, scalar1=1e-5))
        dve(lambda e, qs=qs: e.tensor_copy(out=ROW_RX[:, qs], in_=ROW_SC))
        dve(lambda e: e.reciprocal(out=ROW_SC, in_=ROW_SC), waits=at("dve"))
        dve(lambda e: e.tensor_scalar_mul(out=ROW_SC, in0=ROW_SC, scalar1=127.0))
        srow = cnt["dve"]
        sq_pe = mm(pB[2], k_ones[0:1, :], ROW_SC, True, True, waits=[("dve", srow)])
        for mo in range(8):
            quant_round(xq[mo][:, qs], zq[mo], pB[2], fwv[10][:, :], fwv[11][:, :],
                        waits=[("pe", sq_pe)] if mo == 0 else ())
    p4_done = cnt["act"]
    ckpt("p4", [("xq0", xq[0]), ("xq7", xq[7]), ("rx", ROW_RX)])

    def wmean(src_d, nchunk, width, dst_row):
        dve(lambda e: e.memset(FC[:, 20:21], 0.0))
        lastp = [cnt["dve"]]
        for ck in range(nchunk):
            dk = dma(fw2k[ck % 2][:, 0:width], src_d[ck], waits=[("dve", lastp[0])])
            dve(lambda e, ck=ck: e.tensor_reduce(out=FC[:, 21:22],
                                                 in_=fw2k[ck % 2][:, 0:width],
                                                 axis=AX.X, op=ALU.add,
                                                 apply_absolute_value=True),
                waits=[("dma", dk)])
            dve(lambda e: e.tensor_add(out=FC[:, 20:21], in0=FC[:, 20:21],
                                       in1=FC[:, 21:22]))
            lastp[0] = cnt["dve"]
        t_ = tr(pA[0][0:1, 0:128], FC[:, 20:21], waits=[("dve", lastp[0]), ("act", cnt["act"])])
        dve(lambda e: e.tensor_reduce(out=dst_row, in_=pA[0][0:1, 0:128], axis=AX.X,
                                      op=ALU.add), waits=[("pe", t_)])
        dve(lambda e, dst_row=dst_row: e.tensor_scalar_mul(out=dst_row, in0=dst_row,
                                                           scalar1=1.0 / (DH * C)))
        dve(lambda e, dst_row=dst_row: e.tensor_scalar_max(out=dst_row, in0=dst_row,
                                                           scalar1=1e-5))
        return cnt["dve"]

    w1halves = [w1T_d[128 * (hk // 2):128 * (hk // 2) + 128,
                      2048 * (hk % 2):2048 * (hk % 2) + 2048] for hk in range(16)]
    cm1 = wmean(w1halves, 16, 2048, ROW_CM[:, 0:1])
    w2chunks = [w2T_d[128 * ck:128 * (ck + 1), :] for ck in range(32)]
    cm2 = wmean(w2chunks, 32, 1024, ROW_CM[:, 1:2])
    dve(lambda e: e.reciprocal(out=ROW_CM[:, 2:3], in_=ROW_CM[:, 0:1]),
        waits=[("dve", cm1)])
    dve(lambda e: e.reciprocal(out=ROW_CM[:, 3:4], in_=ROW_CM[:, 1:2]),
        waits=[("dve", cm2)])
    scl = cnt["dve"]
    m_ = mm(pS[0][:, 0:1], k_ones[0:1, :], ROW_CM[:, 2:3], True, True,
            waits=[("dve", scl)])
    dve(lambda e: e.tensor_copy(out=FC[:, 16:17], in_=pS[0][:, 0:1]), waits=[("pe", m_)])
    m_ = mm(pS[0][:, 0:1], k_ones[0:1, :], ROW_CM[:, 3:4], True, True, waits=at("dve"))
    dve(lambda e: e.tensor_copy(out=FC[:, 17:18], in_=pS[0][:, 0:1]), waits=[("pe", m_)])
    swcols = cnt["dve"]
    dve(lambda e: e.tensor_single_scalar(out=ROW_RX, in_=ROW_RX,
                                         scalar=ROW_CM[:, 0:1], op=ALU.mult),
        waits=[("dve", swcols), ("act", p4_done)])
    dve(lambda e: e.tensor_scalar_mul(out=ROW_RX, in0=ROW_RX, scalar1=1.0 / 127.0))
    rx_done = cnt["dve"]

    def wquant(src_getter, nchunk, width, scol, out_write):
        for ck in range(nchunk):
            hp = ck % 2
            dk = dma(fw2k[hp][:, 0:width], src_getter(ck),
                     waits=[("act", cnt["act"])])
            dve(lambda e, hp=hp, width=width, scol=scol: e.tensor_single_scalar(
                out=fw2k[2][:, 0:width], in_=fw2k[hp][:, 0:width], scalar=scol,
                op=ALU.mult), waits=[("dma", dk)])
            dve(lambda e, width=width: e.tensor_scalar_min(
                out=fw2k[2][:, 0:width], in0=fw2k[2][:, 0:width], scalar1=1.0))
            dve(lambda e, width=width: e.tensor_scalar_max(
                out=fw2k[2][:, 0:width], in0=fw2k[2][:, 0:width], scalar1=-1.0))
            act(lambda e, hp=hp, width=width: e.activation(
                out=fw2k[hp][:, 0:width], in_=fw2k[2][:, 0:width], func=AF.Identity,
                bias=k_magic, scale=1.0), waits=at("dve"))
            dve(lambda e, hp=hp, width=width: e.tensor_copy(
                out=fw2k[2].bitcast(I32)[:, 0:width],
                in_=fw2k[hp].bitcast(I32)[:, 0:width]), waits=at("act"))
            act(lambda e, ck=ck, width=width: e.activation(
                out=out_write(ck), in_=fw2k[2][:, 0:width], func=AF.Identity,
                bias=k_negmagic, scale=1.0), waits=at("dve"))

    wquant(lambda hk: w1halves[hk], 16, 2048, FC[:, 16:17],
           lambda hk: w1q[hk // 2][:, 2048 * (hk % 2):2048 * (hk % 2) + 2048])
    w2st_free = [0, 0]
    def w2_write(ck):
        return w2bf[ck % 2]
    for ck in range(32):
        hp = ck % 2
        dk = dma(fw2k[hp][:, 0:1024], w2chunks[ck], waits=[("act", cnt["act"])])
        dve(lambda e, hp=hp: e.tensor_single_scalar(
            out=fw2k[2][:, 0:1024], in_=fw2k[hp][:, 0:1024], scalar=FC[:, 17:18],
            op=ALU.mult), waits=[("dma", dk)])
        dve(lambda e: e.tensor_scalar_min(out=fw2k[2][:, 0:1024],
                                          in0=fw2k[2][:, 0:1024], scalar1=1.0))
        dve(lambda e: e.tensor_scalar_max(out=fw2k[2][:, 0:1024],
                                          in0=fw2k[2][:, 0:1024], scalar1=-1.0))
        act(lambda e, hp=hp: e.activation(out=fw2k[hp][:, 0:1024],
                                          in_=fw2k[2][:, 0:1024], func=AF.Identity,
                                          bias=k_magic, scale=1.0), waits=at("dve"))
        dve(lambda e, hp=hp: e.tensor_copy(out=fw2k[2].bitcast(I32)[:, 0:1024],
                                           in_=fw2k[hp].bitcast(I32)[:, 0:1024]),
            waits=at("act"))
        act(lambda e, ck=ck: e.activation(out=w2bf[ck % 2], in_=fw2k[2][:, 0:1024],
                                          func=AF.Identity, bias=k_negmagic, scale=1.0),
            waits=at("dve") + [("gdma", w2st_free[ck % 2])])
        w2st_free[ck % 2] = gdma(scr_w2[128 * ck:128 * (ck + 1), :], w2bf[ck % 2],
                                 waits=at("act"))
    wq_done = cnt["gdma"]
    ckpt("wq", [("w1q0", w1q[0]), ("w1q7", w1q[7]), ("cm", ROW_CM[:, 0:8]),
                ("sw", FC[:, 16:18])])

    final_dma = 0
    for s in range(4):
        L = 512 * s
        left = max(0, L - 1)
        right = min(T, L + 513)
        Wd = right - left
        of0 = 1 if s == 0 else 0
        Na = min(512, Wd)
        Nb = Wd - Na
        bx = mm(pB[2], k_ones[0:1, :], ROW_RX[:, left:left + Na], True, True,
                waits=[("dve", rx_done), ("gdma", wq_done)] if s == 0 else at("dve"))
        if Nb:
            bx = mm(pB[3][:, 0:Nb], k_ones[0:1, :], ROW_RX[:, left + Na:right],
                    True, True)
        dve(lambda e, Na=Na: e.tensor_copy(out=fwv[6][:, 0:Na], in_=pB[2][:, 0:Na]),
            waits=[("pe", bx)])
        if Nb:
            dve(lambda e, Nb=Nb: e.tensor_copy(out=fwv[7][:, 0:Nb], in_=pB[3][:, 0:Nb]))
        rx_sb = cnt["dve"]
        hz = [0, 0]
        for m in range(32):
            for k in range(8):
                mm(pS[m % 2][:, 0:Na], w1q[k][:, 128 * m:128 * (m + 1)],
                   xq[k][:, left:left + Na], k == 0, k == 7,
                   waits=([("pe", bx), ("act", cnt["act"])] if (m == 0 and k == 0)
                          else ([("act", hz[m % 2])] if k == 0 else ())))
            if Nb:
                for k in range(8):
                    mm(pA[m % 2][:, 0:Nb], w1q[k][:, 128 * m:128 * (m + 1)],
                       xq[k][:, left + Na:right], k == 0, k == 7)
            pm = cnt["pe"]
            if s == 0:
                dve(lambda e: e.memset(h1f[:, 0:1], 0.0))
            if of0 + Wd < 514:
                dve(lambda e, o=of0 + Wd: e.memset(h1f[:, o:514], 0.0))
            dve(lambda e, m=m, of0=of0, Na=Na: e.tensor_tensor(
                out=h1f[:, of0:of0 + Na], in0=pS[m % 2][:, 0:Na], in1=fwv[6][:, 0:Na],
                op=ALU.mult), waits=[("pe", pm), ("dve", rx_sb)] + at("act"))
            if Nb:
                dve(lambda e, m=m, of0=of0, Na=Na, Nb=Nb: e.tensor_tensor(
                    out=h1f[:, of0 + Na:of0 + Na + Nb], in0=pA[m % 2][:, 0:Nb],
                    in1=fwv[7][:, 0:Nb], op=ALU.mult))
            hv = cnt["dve"]
            act(lambda e, m=m: e.activation(out=fwv[0][:, :], in_=h1f[:, 0:512],
                                            func=AF.Identity, scale=k_cw[:, 3 * m:3 * m + 1],
                                            bias=k_cb[:, m:m + 1]), waits=[("dve", hv)])
            act(lambda e, m=m: e.activation(out=fwv[1][:, :], in_=h1f[:, 1:513],
                                            func=AF.Identity,
                                            scale=k_cw[:, 3 * m + 1:3 * m + 2]))
            dve(lambda e: e.tensor_add(out=fwv[0][:, :], in0=fwv[0][:, :],
                                       in1=fwv[1][:, :]), waits=at("act"))
            act(lambda e, m=m: e.activation(out=fwv[1][:, :], in_=h1f[:, 2:514],
                                            func=AF.Identity,
                                            scale=k_cw[:, 3 * m + 2:3 * m + 3]),
                waits=at("dve"))
            dve(lambda e: e.tensor_add(out=fwv[0][:, :], in0=fwv[0][:, :],
                                       in1=fwv[1][:, :]), waits=at("act"))
            act(lambda e, m=m: e.activation(out=hsl[m], in_=fwv[0][:, :], func=AF.Silu,
                                            scale=1.0), waits=at("dve"))
            hz[m % 2] = cnt["act"]
            if m == 0:
                dve(lambda e: e.tensor_copy(out=fwv[2][:, :], in_=hsl[0]),
                    waits=at("act"))
                dve(lambda e: e.tensor_copy(out=fwv[8][:, :], in_=hsl[0]))
            else:
                dve(lambda e, m=m: e.tensor_tensor(out=fwv[2][:, :], in0=fwv[2][:, :],
                                                   in1=hsl[m], op=ALU.max),
                    waits=at("act"))
                dve(lambda e, m=m: e.tensor_tensor(out=fwv[8][:, :], in0=fwv[8][:, :],
                                                  in1=hsl[m], op=ALU.min))
        dve(lambda e: e.tensor_scalar_mul(out=fwv[9][:, :], in0=fwv[8][:, :],
                                          scalar1=-1.0))
        dve(lambda e: e.tensor_tensor(out=fwv[2][:, :], in0=fwv[2][:, :],
                                      in1=fwv[9][:, :], op=ALU.max))
        am = cnt["dve"]
        if s == 0:
            ckpt("ffn1", [("h0", hsl[0]), ("h5", hsl[5]), ("h31", hsl[31]),
                          ("hmax", fwv[2]), ("h1f", h1f), ("rxsb", fwv[6]),
                          ("cw", k_cw), ("cb", k_cb)])
        for j in range(4):
            t1 = tr(pA[0][:, 0:128], fwv[2][:, 128 * j:128 * (j + 1)],
                    waits=[("dve", am)])
            dve(lambda e, j=j: e.tensor_reduce(out=fwv[3][:, j:j + 1],
                                               in_=pA[0][:, 0:128], axis=AX.X,
                                               op=ALU.max), waits=[("pe", t1)])
            t2 = tr(pA[1][0:1, 0:128], fwv[3][:, j:j + 1], waits=at("dve"))
            dve(lambda e, j=j: e.tensor_copy(out=ROW_SC[:, 128 * j:128 * (j + 1)],
                                             in_=pA[1][0:1, 0:128]), waits=[("pe", t2)])
        dve(lambda e: e.tensor_scalar_max(out=ROW_SC, in0=ROW_SC, scalar1=1e-5))
        dve(lambda e: e.tensor_single_scalar(out=ROW_INV, in_=ROW_SC,
                                             scalar=ROW_CM[:, 1:2], op=ALU.mult),
            waits=at("dve"))
        dve(lambda e: e.tensor_scalar_mul(out=ROW_INV, in0=ROW_INV, scalar1=1.0 / 127.0))
        dve(lambda e: e.reciprocal(out=ROW_SC, in_=ROW_SC))
        dve(lambda e: e.tensor_scalar_mul(out=ROW_SC, in0=ROW_SC, scalar1=127.0))
        hs = cnt["dve"]
        hb = mm(pB[2], k_ones[0:1, :], ROW_SC, True, True, waits=[("dve", hs)])
        off = of0 + (L - left)
        for m in range(32):
            quant_round(hsl[m], hsl[m], pB[2], fwv[0][:, :], fwv[1][:, :],
                        waits=[("pe", hb)] if m == 0 else ())
        hq_done = cnt["act"]
        if s == 0:
            ckpt("hq", [("hq0", hsl[0]), ("hq31", hsl[31]), ("rh", ROW_INV[:, 0:512]),
                        ("hsc", ROW_SC[:, 0:512])])
        rb = mm(pS[0], k_ones[0:1, :], ROW_INV[:, 0:512], True, True,
                waits=[("act", hq_done)])
        dve(lambda e: e.tensor_copy(out=fwv[3][:, :], in_=pS[0]), waits=[("pe", rb)])
        rh_sb = cnt["dve"]
        w2dma = [0, 0]
        w2pe = [0, 0]
        stat_first = True
        for blk in range(2):
            for k in range(32):
                w2dma[k % 2] = dma(w2bf[k % 2], scr_w2[128 * k:128 * (k + 1), :],
                                   waits=[("pe", w2pe[k % 2])]
                                   + ([("gdma", wq_done)] if s == 0 and blk == 0
                                      else []))
                for moin in range(4):
                    mo = 4 * blk + moin
                    mm(pB[moin], w2bf[k % 2][:, 128 * mo:128 * (mo + 1)],
                       hsl[k][:, 0:512], k == 0, k == 31,
                       waits=[("dma", w2dma[k % 2])] if moin == 0 else ())
                w2pe[k % 2] = cnt["pe"]
            pf = cnt["pe"]
            for moin in range(4):
                mo = 4 * blk + moin
                dve(lambda e, moin=moin: e.tensor_tensor(out=fwv[4][:, :], in0=pB[moin],
                                                         in1=fwv[3][:, :], op=ALU.mult),
                    waits=[("pe", pf if moin == 0 else cnt["pe"]), ("dve", rh_sb)]
                    + at("dma"))
                dx = dma(fwv[5][:, :], scr_x1[128 * mo:128 * (mo + 1), L:L + 512],
                         waits=())
                dve(lambda e: e.tensor_add(out=fwv[4][:, :], in0=fwv[4][:, :],
                                           in1=fwv[5][:, :]), waits=[("dma", dx)])
                zd = cnt["dve"]
                dma(scr_z2[128 * mo:128 * (mo + 1), L:L + 512], fwv[4][:, :],
                    waits=[("dve", zd)])
                act(lambda e: e.activation(out=fwv[6][:, :], in_=fwv[4][:, :],
                                           func=AF.Square, scale=1.0),
                    waits=[("dve", zd)])
                sq = cnt["act"]
                mm(pA[0][0:1, :], k_onescol, fwv[4][:, :], stat_first, mo == 7,
                   waits=[("act", sq)])
                mm(pA[1][0:1, :], k_onescol, fwv[6][:, :], stat_first, mo == 7)
                stat_first = False
        pst = cnt["pe"]
        if s == 0:
            ckpt("z2", [("z2s", scr_z2[0:128, 0:512]), ("z2b", scr_z2[896:1024, 0:512]),
                        ("x1s", scr_x1[0:128, 0:512]), ("x1b", scr_x1[896:1024, 0:512]),
                        ("hq0", hsl[0]), ("hq31", hsl[31])])
        dve(lambda e: e.tensor_scalar_mul(out=ROW_MEAN, in0=pA[0][0:1, :],
                                          scalar1=1.0 / C), waits=[("pe", pst)])
        dve(lambda e: e.tensor_scalar_mul(out=ROW_INV, in0=pA[1][0:1, :],
                                          scalar1=1.0 / C))
        dve(lambda e: e.tensor_tensor(out=ROW_BB, in0=ROW_MEAN, in1=ROW_MEAN,
                                      op=ALU.mult))
        dve(lambda e: e.tensor_tensor(out=ROW_INV, in0=ROW_INV, in1=ROW_BB,
                                      op=ALU.subtract))
        dve(lambda e: e.tensor_scalar_add(out=ROW_INV, in0=ROW_INV, scalar1=EPS_LN))
        act(lambda e: e.activation(out=ROW_INV, in_=ROW_INV, func=AF.Sqrt, scale=1.0),
            waits=at("dve"))
        dve(lambda e: e.reciprocal(out=ROW_INV, in_=ROW_INV), waits=at("act"))
        dve(lambda e: e.tensor_tensor(out=ROW_BB, in0=ROW_MEAN, in1=ROW_INV,
                                      op=ALU.mult))
        dve(lambda e: e.tensor_scalar_mul(out=ROW_BB, in0=ROW_BB, scalar1=-1.0))
        rws = cnt["dve"]
        mm(pB[2], k_ones[0:1, :], ROW_INV, True, True, waits=[("dve", rws)])
        abp = mm(pB[3], k_ones[0:1, :], ROW_BB, True, True)
        for mo in range(8):
            dx = dma(fwv[5][:, :], scr_z2[128 * mo:128 * (mo + 1), L:L + 512],
                     waits=[("dve", cnt["dve"])])
            dve(lambda e: e.tensor_tensor(out=fwv[4][:, :], in0=fwv[5][:, :],
                                          in1=pB[2], op=ALU.mult),
                waits=[("pe", abp)] + at("dma"))
            dve(lambda e: e.tensor_add(out=fwv[4][:, :], in0=fwv[4][:, :], in1=pB[3]))
            final_dma = dma(out_dram[128 * mo:128 * (mo + 1), L:L + 512], fwv[4][:, :],
                            waits=at("dve"))
    push("vector", [("dma", final_dma)], lambda e: e.memset(FC[:, 63:64], 0.0), "dve", 1)

    assemble()
    return nc


def _prep_inputs(inputs):
    f32 = np.float32
    d = {}
    wq = np.ascontiguousarray(inputs["L0_attn_w_cq"], f32)
    wkv = np.ascontiguousarray(inputs["L0_attn_w_ckv"], f32)
    d["w_sa"] = np.ascontiguousarray(np.concatenate([wq, wkv], axis=1), f32)
    d["w_sb"] = np.ascontiguousarray(np.concatenate(
        [inputs["L0_attn_w_krope"], inputs["L0_attn_w_gate"], inputs["L0_attn_w_imp"]],
        axis=1), f32)
    dqn = np.asarray(inputs["L0_attn_w_dq_nope"], f32).reshape(QL, 16, 32)
    dqr = np.asarray(inputs["L0_attn_w_dq_rope"], f32).reshape(QL, 16, 64)
    parts = []
    for g in range(4):
        parts += [dqn[:, 4 * g:4 * g + 4, :].reshape(QL, 128),
                  dqr[:, 4 * g:4 * g + 4, 0:32].reshape(QL, 128),
                  dqr[:, 4 * g:4 * g + 4, 32:64].reshape(QL, 128)]
    d["wdq"] = np.ascontiguousarray(np.concatenate(parts, axis=1), f32)

    def group_k(w):
        w = np.asarray(w, f32).reshape(C, 16, 96)
        ps = []
        for g in range(4):
            ps += [w[:, 4 * g:4 * g + 4, 0:32].reshape(C, 128),
                   w[:, 4 * g:4 * g + 4, 32:64].reshape(C, 128),
                   w[:, 4 * g:4 * g + 4, 64:96].reshape(C, 128)]
        return np.ascontiguousarray(np.concatenate(ps, axis=1), f32)

    d["wwink"] = group_k(inputs["L0_attn_w_wink"])
    d["wselk"] = group_k(inputs["L0_attn_w_selk"])
    d["wdk"] = np.ascontiguousarray(inputs["L0_attn_w_dk_nope"], f32)
    d["wdv"] = np.ascontiguousarray(inputs["L0_attn_w_dv"], f32)
    d["wwinv"] = np.ascontiguousarray(inputs["L0_attn_w_winv"], f32)
    d["wselv"] = np.ascontiguousarray(inputs["L0_attn_w_selv"], f32)
    d["wproj"] = np.ascontiguousarray(inputs["L0_attn_w_proj"], f32)
    d["w1T"] = np.ascontiguousarray(np.asarray(inputs["L0_ffn_w1"], f32).T)
    d["w2T"] = np.ascontiguousarray(np.asarray(inputs["L0_ffn_w2"], f32).T)
    d["wimp"] = np.ascontiguousarray(np.asarray(inputs["L0_attn_w_imp"], f32).reshape(C, 1))
    d["gq"] = np.ascontiguousarray(np.asarray(inputs["L0_attn_g_qnorm"], f32).reshape(QL, 1))
    d["gkv"] = np.ascontiguousarray(np.asarray(inputs["L0_attn_g_kvnorm"], f32).reshape(KVL, 1))
    cw = np.asarray(inputs["L0_ffn_conv_w"], f32).reshape(32, 128, 3)
    d["cwcol"] = np.ascontiguousarray(cw.transpose(1, 0, 2).reshape(128, 96))
    d["cbcol"] = np.ascontiguousarray(np.asarray(inputs["L0_ffn_conv_b"], f32).reshape(32, 128).T)
    d["c_trow"] = np.arange(T, dtype=f32).reshape(1, T)
    d["c_drow"] = (np.arange(128) % 32).astype(f32).reshape(1, 128)
    d["c_ident"] = np.eye(128, dtype=f32)
    d["c_iota256"] = np.broadcast_to(np.arange(1, 257, dtype=f32)[None, :],
                                     (128, 256)).copy()
    d["c_tcol"] = np.broadcast_to((16.0 * np.arange(128, dtype=f32))[:, None],
                                  (128, 16)).copy()
    import ml_dtypes
    mk = np.zeros((128, 2048), np.float32)
    p = np.arange(128)[:, None]
    j = np.arange(512)[None, :]
    for i in range(4):
        mk[:, 512 * i:512 * (i + 1)] = (j >= p + 128 * i).astype(f32)
    d["c_masks"] = mk.astype(ml_dtypes.bfloat16)
    d["c_ones"] = np.ones((1, 128), f32)
    return d


def kernel(**inputs):
    if "nc" not in _CACHE:
        _CACHE["nc"] = build_program()
    nc = _CACHE["nc"]
    shared = _prep_inputs(inputs)
    x = np.asarray(inputs["L0_attn_x"], np.float32)
    in_maps = []
    for b in range(8):
        m = dict(shared)
        m["x_tok"] = np.ascontiguousarray(x[b])
        m["xT"] = np.ascontiguousarray(x[b].T)
        in_maps.append(m)
    res = run_bass_kernel_spmd(nc, in_maps, core_ids=list(range(8)))
    out = np.stack([np.asarray(r["outT"]).T for r in res.results], axis=0)
    return out.astype(np.float32)
